# version 11 (final)
# speedup vs baseline: 1.7970x; 1.0186x over previous
import os
import numpy as np
import ml_dtypes
import concourse.bass as bass
import concourse.mybir as mybir
from concourse.bass_utils import run_bass_kernel_spmd
from concourse.alu_op_type import AluOpType as OP
from contextlib import ExitStack

os.environ.setdefault("TRN_ALLOW_ICI_ALL_TO_ALL", "1")

F32 = mybir.dt.float32
BF16 = mybir.dt.bfloat16
I8 = mybir.dt.int8
AF = mybir.ActivationFunctionType
AX = mybir.AxisListType

NCOR = 8
S = 8192
D = 2048
TPC = S // NCOR
DCH = D // 128
INTER = 8192
EPS_LN = 1e-5
RMS_EPS = 1.1920928955078125e-07
MAGIC = 12582912.0
PI = float(np.pi)
PI_CLAMP = float(np.pi * (1.0 - 1e-6))

PHASES = tuple(os.environ.get("A_PHASES", "attn_a,ffn1,attn_b,ffn2").split(","))

ENGS = ("t", "s", "v", "g", "y")


class PB:

    def __init__(self, nc, ctx, bl):
        self.nc, self.bl = nc, bl
        self.sems = {E: [ctx.enter_context(nc.semaphore("sem_" + E)), 0] for E in ENGS}
        self.dsems = {E: [ctx.enter_context(nc.semaphore("dsem_" + E)), 0] for E in ENGS}
        self.ccsem = [ctx.enter_context(nc.semaphore("sem_cc")), 0]
        self.q = {E: [] for E in ENGS}

    def step(self, E, fn, after=(), n_dma=0, cc=False):
        sem = self.ccsem if cc else (self.dsems[E] if n_dma else self.sems[E])
        inc_total = 1 if (cc or not n_dma) else 16 * n_dma
        waits = {}
        for tok in after:
            if tok is None:
                continue
            semref, val = tok
            key = id(semref)
            if key not in waits or waits[key][1] < val:
                waits[key] = (semref, val)
        sem[1] += inc_total
        self.q[E].append((fn, list(waits.values()), sem, inc_total, n_dma, cc))
        return (sem, sem[1])

    def flush(self):
        secmap = {"t": self.bl.tensor, "s": self.bl.scalar, "v": self.bl.vector,
                  "g": self.bl.gpsimd, "y": self.bl.sync}
        for E in ENGS:
            steps = self.q[E]
            self.q[E] = []
            if not steps:
                continue

            def make(steps=steps, E=E):
                def body(eng):
                    for (fn, waits, sem, inc_total, n_dma, cc) in steps:
                        for (semref, val) in waits:
                            eng.wait_ge(semref[0], val)
                        got = {"n": 0}
                        defk = 16 if n_dma else 1

                        def fin(instr, kk=None, _sem=sem, _got=got, _defk=defk):
                            k = _defk if kk is None else kk
                            instr.then_inc(_sem[0], k)
                            _got["n"] += k

                        fn(eng, fin)
                        assert got["n"] == inc_total, \
                            f"eng {E}: inc mismatch {got['n']} != {inc_total}"
                return body

            secmap[E](make())


class Env:
    pass


def build_program():
    nc = bass.Bass("TRN2", target_bir_lowering=False, detect_race_conditions=False)
    e = Env()
    e.nc = nc
    ctx = ExitStack()

    dt = nc.dram_tensor
    e.xT_in = dt("xT_in", [D, TPC], F32, kind="ExternalInput").ap()
    e.pos_in = dt("pos_in", [1, TPC], F32, kind="ExternalInput").ap()
    e.xT_out = dt("xT_out", [D, TPC], F32, kind="ExternalOutput").ap()
    for nm, shp in (("wqa_t", [D, D]), ("wqa_st", [D, D]),
                    ("wka_t", [512, D]), ("wka_st", [512, D]),
                    ("wva_t", [128, 16 * 512]), ("woa_t", [512, 16 * 512]),
                    ("wqb_t", [512, 16 * 512]), ("wkb_t", [512, 16 * 512]),
                    ("wvb_t", [512, 16 * 512]), ("wob_t", [512, 16 * 512])):
        setattr(e, nm, dt(nm, shp, F32, kind="ExternalInput").ap())
    for nm in ("lam_q1", "lam_q2", "lam_k1", "lam_k2"):
        setattr(e, nm, dt(nm, [1, 128], F32, kind="ExternalInput").ap())
    e.w_gate = dt("w_gate", [32 * 128, 16 * 256], I8, kind="ExternalInput").ap()
    e.w_up = dt("w_up", [32 * 128, 16 * 256], I8, kind="ExternalInput").ap()
    e.w_down = dt("w_down", [16 * 128, 64 * 128], I8, kind="ExternalInput").ap()
    e.s_gate = dt("s_gate", [64, 128], F32, kind="ExternalInput").ap()
    e.s_up = dt("s_up", [64, 128], F32, kind="ExternalInput").ap()
    e.s_down = dt("s_down", [16, 128], F32, kind="ExternalInput").ap()
    e.a2aq_i = dt("a2aq_i", [D, TPC], BF16, kind="Internal").ap()
    e.a2aq_o = dt("a2aq_o", [D, TPC], BF16, kind="Internal").ap()
    e.a2ak_i = dt("a2ak_i", [512, TPC], BF16, kind="Internal").ap()
    e.a2ak_o = dt("a2ak_o", [512, TPC], BF16, kind="Internal").ap()
    e.a2av_i = dt("a2av_i", [S, 64], BF16, kind="Internal").ap()
    e.a2av_o = dt("a2av_o", [S, 64], BF16, kind="Internal").ap()
    e.a2ao_i = dt("a2ao_i", [D, TPC], BF16, kind="Internal").ap()
    e.a2ao_o = dt("a2ao_o", [D, TPC], BF16, kind="Internal").ap()
    e.qb_dram = dt("qb_dram", [D, TPC], BF16, kind="Internal").ap()
    e.agk_i = dt("agk_i", [D, TPC], BF16, kind="Internal").ap()
    e.agk_o = dt("agk_o", [NCOR * D, TPC], BF16, kind="Internal").ap()
    e.agv_iA = dt("agv_iA", [TPC, 1024], BF16, kind="Internal").ap()
    e.agv_iB = dt("agv_iB", [TPC, 1024], BF16, kind="Internal").ap()
    e.agv_oA = dt("agv_oA", [S, 1024], BF16, kind="Internal").ap()
    e.agv_oB = dt("agv_oB", [S, 1024], BF16, kind="Internal").ap()
    e.a2an_i = dt("a2an_i", [S, 256], BF16, kind="Internal").ap()
    e.a2an_o = dt("a2an_o", [S, 256], BF16, kind="Internal").ap()

    sb = lambda name, shape, dtp: ctx.enter_context(nc.sbuf_tensor(name, shape, dtp))
    e.xT = sb("xT", [128, DCH * TPC], F32)
    e.hT = sb("hT", [128, DCH * TPC], BF16)
    e.ones_row = sb("ones_row", [128, 128], F32)
    e.ident64 = sb("ident64", [64, 64], F32)
    e.sg_sb = sb("sg_sb", [128, 64], F32)
    e.su_sb = sb("su_sb", [128, 64], F32)
    e.sd_sb = sb("sd_sb", [128, 16], F32)
    e.neg_lam = sb("neg_lam", [128, 1], F32)
    e.cbias = sb("cbias", [128, 4], F32)
    e.ones_bf = sb("ones_bf", [128, 1], BF16)
    e.stat = sb("stat", [1, 1024], F32)

    bl = ctx.enter_context(nc.Block())
    pb = PB(nc, ctx, bl)
    e.pb = pb
    e.bl = bl
    e.ctx = ctx

    if "attn_a" in PHASES:
        with (nc.sbuf_tensor("cosq", [128, TPC], F32) as cosq,
              nc.sbuf_tensor("sinq", [128, TPC], F32) as sinq,
              nc.sbuf_tensor("ropescr", [128, 2048], F32) as ropescr):
            e.cosq, e.sinq, e.ropescr = cosq, sinq, ropescr
            emit_boot(e)
            emit_ln(e, "ln1")
            emit_attn_a(e)
    else:
        e.cosq = e.sinq = e.ropescr = None
        emit_boot(e)
    if "ln_dbg" in PHASES:
        emit_ln(e, "lnd")
        emit_hcopy(e)
    if "ffn1" in PHASES:
        emit_ln(e, "ln2")
        emit_ffn(e, "ffn1")
    if "attn_b" in PHASES:
        emit_ln(e, "ln3")
        emit_attn_b(e)
    if "ffn2" in PHASES:
        emit_ln(e, "ln4")
        emit_ffn(e, "ffn2")
    emit_store(e)
    pb.flush()

    ctx.close()
    return nc


def emit_boot(e):
    pb, nc = e.pb, e.nc
    bootctx = ExitStack()
    e.bootscr = bootctx.enter_context(nc.sbuf_tensor("bootscr", [64, 448], F32))

    def ld_x(eng, fin):
        for d in range(DCH):
            fin(eng.dma_start(out=e.xT[:, d * TPC:(d + 1) * TPC],
                              in_=e.xT_in[d * 128:(d + 1) * 128, :]))
        if "attn_a" in PHASES:
            fin(eng.dma_start(out=e.cosq[0:1, 0:1024], in_=e.pos_in))
        fin(eng.dma_start(out=e.bootscr[0:64, 0:128], in_=e.s_gate))
        fin(eng.dma_start(out=e.bootscr[0:64, 128:256], in_=e.s_up))
        fin(eng.dma_start(out=e.bootscr[0:16, 256:384], in_=e.s_down))
        for i, nm in enumerate(("lam_q1", "lam_k1", "lam_q2", "lam_k2")):
            fin(eng.dma_start(out=e.stat[0:1, i * 128:(i + 1) * 128], in_=getattr(e, nm)))
    e.tok_x = pb.step("y", ld_x, n_dma=DCH + 3 + 4 + (1 if "attn_a" in PHASES else 0))
    e.tok_lam_in = e.tok_x

    def consts_g0(eng, fin):
        fin(eng.iota(e.bootscr[0:64, 384:448], pattern=[[1, 64]], base=0, channel_multiplier=-1,
                     allow_small_or_imprecise_dtypes=True))
    tok_c0 = pb.step("g", consts_g0)

    def consts_g(eng, fin):
        fin(eng.tensor_scalar(e.ident64[:], e.bootscr[0:64, 384:448], 0.0, None, OP.is_equal))
    e.tok_consts = pb.step("g", consts_g, after=[tok_c0])

    def ones_v(eng, fin):
        eng.memset(e.ones_row[:], 1.0)
        eng.memset(e.ones_bf[:], 1.0)
        eng.memset(e.cbias[:, 0:1], EPS_LN)
        fin(eng.memset(e.cbias[:, 1:2], float(np.log(0.2))))
    e.tok_ones = pb.step("v", ones_v)

    with ExitStack() as bctx:
        en = bctx.enter_context
        ps1 = en(nc.psum_tensor("b_ps1", [128, 64], F32))
        ps2 = en(nc.psum_tensor("b_ps2", [128, 64], F32))
        ps3 = en(nc.psum_tensor("b_ps3", [128, 16], F32))
        ps_ang = en(nc.psum_tensor("b_ang", [128, TPC], F32))
        def sc_tr(eng, fin):
            eng.transpose(ps1[:], e.bootscr[0:64, 0:128], e.ident64[:])
            eng.transpose(ps2[:], e.bootscr[0:64, 128:256], e.ident64[:])
            fin(eng.transpose(ps3[:], e.bootscr[0:16, 256:384], e.ident64[0:16, 0:16]))
        tok_tr = pb.step("t", sc_tr, after=[e.tok_x, e.tok_consts])

        def sc_cp(eng, fin):
            eng.tensor_copy(e.sg_sb[:], ps1[:])
            eng.tensor_copy(e.su_sb[:], ps2[:])
            fin(eng.tensor_copy(e.sd_sb[:], ps3[:]))
        e.tok_scales = pb.step("v", sc_cp, after=[tok_tr])

        if "attn_a" in PHASES:
            def invf_iota(eng, fin):
                fin(eng.iota(e.stat[0:1, 512:640], pattern=[[0, 2], [1, 32], [0, 2]],
                             base=0, channel_multiplier=0,
                             allow_small_or_imprecise_dtypes=True))
            tok_ifi = pb.step("g", invf_iota, after=[e.tok_consts])

            def invf(eng, fin):
                fin(eng.activation(e.stat[0:1, 640:768], e.stat[0:1, 512:640], AF.Exp,
                                   scale=float(-2.0 * np.log(10000.0) / 64.0)))
            tok_if = pb.step("s", invf, after=[tok_ifi])

            def altc(eng, fin):
                fin(eng.iota(e.cbias[:, 2:3], pattern=[[0, 1]], base=0, channel_multiplier=1,
                             allow_small_or_imprecise_dtypes=True))
            tok_alt1 = pb.step("g", altc, after=[e.tok_consts])

            p_col = e.cbias[:, 2:3]
            y_col = e.ropescr[:, 0:1]
            t_a = pb.step("v", lambda g_, fin: fin(g_.tensor_scalar(
                y_col, p_col, 0.5, -0.25, OP.mult, OP.add)), after=[tok_alt1, e.tok_ones])
            t_a = pb.step("v", lambda g_, fin: fin(g_.tensor_scalar(
                y_col, y_col, MAGIC, None, OP.add)), after=[t_a])
            t_a = pb.step("v", lambda g_, fin: fin(g_.tensor_scalar(
                y_col, y_col, MAGIC, None, OP.subtract)), after=[t_a])
            t_a = pb.step("v", lambda g_, fin: fin(g_.scalar_tensor_tensor(
                y_col, y_col, -2.0, p_col, OP.mult, OP.add)), after=[t_a])
            tok_alt = pb.step("v", lambda g_, fin: fin(g_.tensor_scalar(
                e.cbias[:, 3:4], y_col, 2.0, -1.0, OP.mult, OP.add)), after=[t_a])

            def angmm(eng, fin):
                ins = None
                for hf in range(2):
                    ins = eng.matmul(ps_ang[:, hf * 512:(hf + 1) * 512], e.stat[0:1, 640:768],
                                     e.cosq[0:1, hf * 512:(hf + 1) * 512], start=True, stop=True)
                fin(ins)
            tok_ang = pb.step("t", angmm, after=[tok_if, e.tok_x, tok_tr])

            def rr1(eng, fin):
                a = ps_ang[:]
                y, n = e.ropescr[:, 0:1024], e.ropescr[:, 1024:2048]
                eng.tensor_scalar(y, a, float(1 / (2 * PI)), MAGIC, OP.mult, OP.add)
                eng.tensor_scalar(n, y, MAGIC, None, OP.subtract)
                eng.scalar_tensor_tensor(y, n, float(-2 * PI), a, OP.mult, OP.add)
                fin(eng.tensor_scalar(e.sinq[:], y, PI_CLAMP, -PI_CLAMP, OP.min, OP.max))
            tok_rr1 = pb.step("v", rr1, after=[tok_ang, e.tok_scales, tok_alt])

            def dosin1(eng, fin):
                fin(eng.activation(e.sinq[:], e.sinq[:], AF.Sin))
            tok_sin1 = pb.step("s", dosin1, after=[tok_rr1])

            def rr2(eng, fin):
                a = ps_ang[:]
                y, n = e.ropescr[:, 0:1024], e.ropescr[:, 1024:2048]
                eng.tensor_scalar(e.sinq[:], e.sinq[:], e.cbias[:, 3:4], None, OP.mult)
                eng.tensor_scalar(y, a, float(1 / (2 * PI)), 0.25, OP.mult, OP.add)
                eng.tensor_scalar(y, y, MAGIC, None, OP.add)
                eng.tensor_scalar(n, y, MAGIC, None, OP.subtract)
                eng.scalar_tensor_tensor(y, n, float(-2 * PI), a, OP.mult, OP.add)
                eng.tensor_scalar(y, y, float(PI / 2), None, OP.add)
                fin(eng.tensor_scalar(e.cosq[:], y, PI_CLAMP, -PI_CLAMP, OP.min, OP.max))
            tok_rr2 = pb.step("v", rr2, after=[tok_sin1])

            def dosin2(eng, fin):
                fin(eng.activation(e.cosq[:], e.cosq[:], AF.Sin))
            e.tok_rope = pb.step("s", dosin2, after=[tok_rr2])
        else:
            e.tok_rope = None
        pb.flush()
    bootctx.close()

    e.tok_hfree = None
    e.tok_scr_free = e.tok_rope
    e.tok_psum_free = [e.tok_scales, e.tok_rope]


def emit_ln(e, tag):
    pb, nc = e.pb, e.nc
    with ExitStack() as lctx:
        en = lctx.enter_context
        ps_s1 = en(nc.psum_tensor(tag + "_s1", [1, 1024], F32))
        ps_s2 = en(nc.psum_tensor(tag + "_s2", [1, 1024], F32))
        ps_ab = en(nc.psum_tensor(tag + "_ab", [128, 1024], F32))
        ps_bb = en(nc.psum_tensor(tag + "_bb", [128, 1024], F32))
        lntmp = en(nc.sbuf_tensor(tag + "_tmp", [128, 1024], F32))
        lnrows = en(nc.sbuf_tensor(tag + "_rows", [1, 4096], F32))
        sq = [en(nc.sbuf_tensor(tag + "_sq1", [128, 1024], BF16)),
              en(nc.sbuf_tensor(tag + "_sq2", [128, 1024], BF16))]
        tok_sq = [None] * DCH
        tok_s2 = [None] * DCH
        for d in range(DCH):
            def fsq(eng, fin, d=d):
                fin(eng.activation(sq[d % 2][:], e.xT[:, d * TPC:(d + 1) * TPC], AF.Square))
            deps = [e.tok_x, e.tok_scr_free]
            if d >= 2:
                deps.append(tok_s2[d - 2])
            tok_sq[d] = pb.step("s", fsq, after=deps)

            def fs1(eng, fin, d=d):
                for hf in range(2):
                    eng.matmul(ps_s1[:, hf * 512:(hf + 1) * 512], e.ones_row[:, 0:1],
                               e.xT[:, d * TPC + hf * 512: d * TPC + (hf + 1) * 512],
                               start=(d == 0), stop=(d == DCH - 1))
                ins = None
                for hf in range(2):
                    ins = eng.matmul(ps_s2[:, hf * 512:(hf + 1) * 512], e.ones_bf[:, 0:1],
                                     sq[d % 2][:, hf * 512:(hf + 1) * 512],
                                     start=(d == 0), stop=(d == DCH - 1))
                fin(ins)
            fdep = [e.tok_x, e.tok_ones, tok_sq[d]]
            if d == 0:
                fdep += e.tok_psum_free
            tok_s2[d] = pb.step("t", fs1, after=fdep)

        R_a = lnrows[0:1, 0:1024]
        R_m = lnrows[0:1, 1024:2048]
        R_b = lnrows[0:1, 2048:3072]
        R_q = lnrows[0:1, 3072:4096]

        def fstat1(eng, fin):
            eng.tensor_scalar(R_m, ps_s1[:], float(1.0 / D), None, OP.mult)
            eng.tensor_scalar(R_q, ps_s2[:], float(1.0 / D), None, OP.mult)
            eng.tensor_tensor(R_a, R_m, R_m, OP.mult)
            fin(eng.tensor_tensor(R_q, R_q, R_a, OP.subtract))
        tok_v1 = pb.step("v", fstat1, after=[tok_s2[DCH - 1]])

        def fstat2(eng, fin):
            eng.activation(R_b, R_q, AF.Ln, bias=e.cbias[0:1, 0:1])
            fin(eng.activation(R_a, R_b, AF.Exp, scale=-0.5))
        tok_v2 = pb.step("s", fstat2, after=[tok_v1])

        def fstat3(eng, fin):
            eng.tensor_tensor(R_b, R_m, R_a, OP.mult)
            fin(eng.tensor_scalar(R_b, R_b, -1.0, None, OP.mult))
        tok_v3 = pb.step("v", fstat3, after=[tok_v2])

        def fbc(eng, fin):
            ins = None
            for hf in range(2):
                eng.matmul(ps_ab[:, hf * 512:(hf + 1) * 512], e.ones_row[0:1, 0:128],
                           R_a[:, hf * 512:(hf + 1) * 512], start=True, stop=True)
                ins = eng.matmul(ps_bb[:, hf * 512:(hf + 1) * 512], e.ones_row[0:1, 0:128],
                                 R_b[:, hf * 512:(hf + 1) * 512], start=True, stop=True)
            fin(ins)
        tok_bc = pb.step("t", fbc, after=[tok_v3])

        tok_hh = None
        for d in range(DCH):
            def fh(eng, fin, d=d):
                sl = slice(d * TPC, (d + 1) * TPC)
                eng.tensor_tensor(lntmp[:], e.xT[:, sl], ps_ab[:], OP.mult)
                fin(eng.tensor_tensor(e.hT[:, sl], lntmp[:], ps_bb[:], OP.add))
            tok_hh = pb.step("v", fh, after=[tok_bc, e.tok_hfree])
        e.tok_h = tok_hh
        e.tok_scr_free = tok_hh
        e.tok_psum_free = [tok_hh]
        pb.flush()


def emit_ffn(e, tag):
    pb, nc = e.pb, e.nc
    NB = 32
    THW = 256
    NTH = TPC // THW
    with ExitStack() as fctx:
        en = fctx.enter_context
        pg0 = en(nc.psum_tensor(tag + "_g0", [128, THW], F32))
        pg1 = en(nc.psum_tensor(tag + "_g1", [128, THW], F32))
        pu0 = en(nc.psum_tensor(tag + "_u0", [128, THW], F32))
        pu1 = en(nc.psum_tensor(tag + "_u1", [128, THW], F32))
        py0 = en(nc.psum_tensor(tag + "_y0", [128, THW], F32))
        py1 = en(nc.psum_tensor(tag + "_y1", [128, THW], F32))
        fuseT = en(nc.sbuf_tensor(tag + "_fuse", [128, 64 * THW], BF16))
        wg0 = en(nc.sbuf_tensor(tag + "_wg0", [128, 16 * 256], BF16))
        wg1 = en(nc.sbuf_tensor(tag + "_wg1", [128, 16 * 256], BF16))
        wu0 = en(nc.sbuf_tensor(tag + "_wu0", [128, 16 * 256], BF16))
        wu1 = en(nc.sbuf_tensor(tag + "_wu1", [128, 16 * 256], BF16))
        wd0 = en(nc.sbuf_tensor(tag + "_wd0", [128, 64 * 128], BF16))
        wd1 = en(nc.sbuf_tensor(tag + "_wd1", [128, 64 * 128], BF16))
        gate_bf = en(nc.sbuf_tensor(tag + "_gb", [128, 2 * THW], BF16))
        pgs, pus, pys = [pg0, pg1], [pu0, pu1], [py0, py1]
        wgs, wus, wds = [wg0, wg1], [wu0, wu1], [wd0, wd1]
        tok_band = {}
        tok_wd = {}
        tok_up = {}
        tok_fuse = {}
        tok_silu = {}
        tok_down = {}
        tok_xupd = {}

        def band_load(th, b):
            def fldb(eng, fin, b=b):
                fin(eng.dma_start(out=wgs[b % 2][:],
                                  in_=e.w_gate[b * 128:(b + 1) * 128, :]))
                fin(eng.dma_start(out=wus[b % 2][:],
                                  in_=e.w_up[b * 128:(b + 1) * 128, :]))
            lin = th * NB + b
            deps = [e.tok_h]
            if lin >= 2:
                lb2 = lin - 2
                deps.append(tok_up[(lb2 // NB, (lb2 % NB) * 2 + 1)])
            tok_band[(th, b)] = pb.step("g", fldb, after=deps, n_dma=2)

        def wd_load(th, dc):
            def fldd(eng, fin, dc=dc):
                fin(eng.dma_start(out=wds[dc % 2][:],
                                  in_=e.w_down[dc * 128:(dc + 1) * 128, :]))
            lin = th * 16 + dc
            deps = []
            if lin < 2:
                deps.append(e.tok_h)
            if lin >= 2:
                pl = lin - 2
                deps.append(tok_down[(pl // 16, pl % 16)])
            tok_wd[(th, dc)] = pb.step("g", fldd, after=deps, n_dma=1)

        for th in range(NTH):
            if th == 0:
                band_load(0, 0)
                band_load(0, 1)
            for b in range(NB):
                for ic in range(2):
                    c = b * 2 + ic
                    lc = th * 64 + c

                    def fmm(eng, fin, th=th, b=b, ic=ic, c=c):
                        pg, pu = pgs[c % 2], pus[c % 2]
                        wgb, wub = wgs[b % 2], wus[b % 2]
                        for d in range(DCH):
                            lsl = slice(d * 256 + ic * 128, d * 256 + (ic + 1) * 128)
                            tsl = slice(d * TPC + th * THW, d * TPC + (th + 1) * THW)
                            eng.matmul(pg[:], wgb[:, lsl], e.hT[:, tsl],
                                       start=(d == 0), stop=(d == DCH - 1))
                        ins = None
                        for d in range(DCH):
                            lsl = slice(d * 256 + ic * 128, d * 256 + (ic + 1) * 128)
                            tsl = slice(d * TPC + th * THW, d * TPC + (th + 1) * THW)
                            ins = eng.matmul(pu[:], wub[:, lsl], e.hT[:, tsl],
                                             start=(d == 0), stop=(d == DCH - 1))
                        fin(ins)
                    deps = [tok_band[(th, b)], e.tok_h]
                    if lc < 2:
                        deps += e.tok_psum_free
                    if lc >= 2:
                        pl = lc - 2
                        deps += [tok_fuse[(pl // 64, pl % 64)], tok_silu[(pl // 64, pl % 64)]]
                    tok_up[(th, c)] = pb.step("t", fmm, after=deps)

                    def fsilu(eng, fin, th=th, c=c):
                        fin(eng.activation(gate_bf[:, (c % 2) * THW:((c % 2) + 1) * THW],
                                           pgs[c % 2][:], AF.Silu, scale=e.sg_sb[:, c:c + 1]))
                    tok_silu[(th, c)] = pb.step("s", fsilu,
                                                after=[tok_up[(th, c)], e.tok_scales])

                    def ffuse(eng, fin, th=th, c=c):
                        fin(eng.scalar_tensor_tensor(
                            fuseT[:, c * THW:(c + 1) * THW], pus[c % 2][:],
                            e.su_sb[:, c:c + 1],
                            gate_bf[:, (c % 2) * THW:((c % 2) + 1) * THW],
                            OP.mult, OP.mult))
                    fdeps = [tok_silu[(th, c)]]
                    if th > 0 and c < 2:
                        fdeps.append(tok_down[(th - 1, 15)])
                    tok_fuse[(th, c)] = pb.step("v", ffuse, after=fdeps)
                nl = th * NB + b + 2
                if 2 <= nl < NTH * NB:
                    band_load(nl // NB, nl % NB)

            if th == 0:
                wd_load(0, 0)
                wd_load(0, 1)
            for dc in range(16):

                def fdn(eng, fin, th=th, dc=dc):
                    py = pys[dc % 2]
                    wdb = wds[dc % 2]
                    ins = None
                    for ic in range(64):
                        ins = eng.matmul(py[:], wdb[:, ic * 128:(ic + 1) * 128],
                                         fuseT[:, ic * THW:(ic + 1) * THW],
                                         start=(ic == 0), stop=(ic == 63))
                    fin(ins)
                deps = [tok_wd[(th, dc)], tok_fuse[(th, 63)]]
                lin = th * 16 + dc
                if lin >= 2:
                    pl = lin - 2
                    deps.append(tok_xupd[(pl // 16, pl % 16)])
                tok_down[(th, dc)] = pb.step("t", fdn, after=deps)

                def fxu(eng, fin, th=th, dc=dc):
                    xs = e.xT[:, dc * TPC + th * THW: dc * TPC + (th + 1) * THW]
                    fin(eng.scalar_tensor_tensor(xs, pys[dc % 2][:], e.sd_sb[:, dc:dc + 1],
                                                 xs, OP.mult, OP.add))
                tok_xupd[(th, dc)] = pb.step("v", fxu, after=[tok_down[(th, dc)], e.tok_x])
                nl = th * 16 + dc + 2
                if 2 <= nl < NTH * 16:
                    wd_load(nl // 16, nl % 16)

        e.tok_x = tok_xupd[(NTH - 1, 15)]
        e.tok_hfree = tok_up[(NTH - 1, 63)]
        e.tok_psum_free = [tok_silu[(NTH - 1, 63)], tok_fuse[(NTH - 1, 63)], e.tok_x]
        pb.flush()


def emit_attn_a(e):
    pb, nc = e.pb, e.nc
    with ExitStack() as a1ctx:
        en = a1ctx.enter_context
        qT = en(nc.sbuf_tensor("a_qT", [128, DCH * TPC], BF16))
        kT = en(nc.sbuf_tensor("a_kT", [128, 4 * TPC], BF16))
        wA0 = en(nc.sbuf_tensor("a_wA0", [128, 16 * 128], BF16))
        wA1 = en(nc.sbuf_tensor("a_wA1", [128, 16 * 128], BF16))
        wB0 = en(nc.sbuf_tensor("a_wB0", [128, 16 * 128], BF16))
        wB1 = en(nc.sbuf_tensor("a_wB1", [128, 16 * 128], BF16))
        wV = en(nc.sbuf_tensor("a_wV", [128, 16 * 512], BF16))
        stg_v = en(nc.sbuf_tensor("a_stv", [128, 2 * 512], BF16))
        psA0 = en(nc.psum_tensor("a_pA0", [128, 512], F32))
        psA1 = en(nc.psum_tensor("a_pA1", [128, 512], F32))
        psB0 = en(nc.psum_tensor("a_pB0", [128, 512], F32))
        psB1 = en(nc.psum_tensor("a_pB1", [128, 512], F32))
        wAs, wBs = [wA0, wA1], [wB0, wB1]
        psAs, psBs = [psA0, psA1], [psB0, psB1]
        NSG = 20
        tok_w = {}
        tok_proj = {}
        tok_rope = {}

        def wload(sg):
            if sg < 16:
                srcA, srcB, r0 = e.wqa_t, e.wqa_st, sg * 128
            else:
                srcA, srcB, r0 = e.wka_t, e.wka_st, (sg - 16) * 128

            def fw(eng, fin, srcA=srcA, srcB=srcB, r0=r0, sg=sg):
                fin(eng.dma_start(out=wAs[sg % 2][:], in_=srcA[r0:r0 + 128, :]))
                fin(eng.dma_start(out=wBs[sg % 2][:], in_=srcB[r0:r0 + 128, :]))
            deps = []
            if sg < 2:
                deps.append(e.tok_h)
            if sg >= 2:
                deps.append(tok_proj[(sg - 2, 1)])
            tok_w[sg] = pb.step("g", fw, after=deps, n_dma=2)

        def wvload(eng, fin):
            fin(eng.dma_start(out=wV[:], in_=e.wva_t))
        tok_wv = pb.step("g", wvload, after=[e.tok_h], n_dma=1)

        wload(0)
        wload(1)
        for sg in range(NSG):
            dst = qT if sg < 16 else kT
            for th in range(2):
                u = th

                def fp(eng, fin, sg=sg, th=th, u=u):
                    pA, pB = psAs[u % 2], psBs[u % 2]
                    tsl = lambda d: slice(d * TPC + th * 512, d * TPC + (th + 1) * 512)
                    lsl = lambda d: slice(d * 128, (d + 1) * 128)
                    for d in range(DCH):
                        eng.matmul(pA[:], wAs[sg % 2][:, lsl(d)], e.hT[:, tsl(d)],
                                   start=(d == 0), stop=(d == DCH - 1))
                    ins = None
                    for d in range(DCH):
                        ins = eng.matmul(pB[:], wBs[sg % 2][:, lsl(d)], e.hT[:, tsl(d)],
                                         start=(d == 0), stop=(d == DCH - 1))
                    fin(ins)
                deps = [tok_w[sg], e.tok_h]
                if sg == 0:
                    deps += e.tok_psum_free
                prev = (sg, u - 2) if u >= 2 else ((sg - 1, u) if sg > 0 else None)
                if prev in tok_rope:
                    deps.append(tok_rope[prev])
                tok_proj[(sg, u)] = pb.step("t", fp, after=deps)

                def fr(eng, fin, sg=sg, th=th, u=u, dst=dst):
                    pA, pB = psAs[u % 2], psBs[u % 2]
                    csl = slice(th * 512, (th + 1) * 512)
                    cos, sinp = e.cosq[:, csl], e.sinq[:, csl]
                    dqc = sg if sg < 16 else sg - 16
                    col0 = dqc * TPC + th * 512
                    t1 = e.ropescr[:, 0:512]
                    t2 = e.ropescr[:, 512:1024]
                    eng.tensor_tensor(t1, pA[:], cos, OP.mult)
                    eng.tensor_tensor(t2, pB[:], sinp, OP.mult)
                    fin(eng.tensor_tensor(dst[:, col0:col0 + 512], t1, t2, OP.add))
                tok_rope[(sg, u)] = pb.step("v", fr,
                                            after=[tok_proj[(sg, u)], e.tok_rope])
            if sg + 2 < NSG:
                wload(sg + 2)

        tok_vst = {}
        prev_vev = None
        for kt in range(8):
            def fv(eng, fin, kt=kt):
                pq = psAs[kt % 2]
                ins = None
                for d in range(DCH):
                    ins = eng.matmul(pq[:], e.hT[:, d * TPC + kt * 128: d * TPC + (kt + 1) * 128],
                                     wV[:, d * 512:(d + 1) * 512],
                                     start=(d == 0), stop=(d == DCH - 1))
                fin(ins)
            deps = [tok_wv, e.tok_h, tok_rope[(NSG - 1, 1)], tok_rope[(NSG - 1, 0)]]
            if kt >= 2:
                deps.append(tok_vst[kt - 2])
            tok_vp = pb.step("t", fv, after=deps)

            def fve(eng, fin, kt=kt):
                fin(eng.activation(stg_v[:, (kt % 2) * 512:(kt % 2 + 1) * 512],
                                   pqs_dummy[kt % 2][:] if False else psAs[kt % 2][:],
                                   AF.Copy))
            edeps = [tok_vp]
            if prev_vev is not None and kt >= 2:
                edeps.append(tok_vst[kt - 2])
            tok_vev = pb.step("s", fve, after=edeps)
            prev_vev = tok_vev

            def fvs(eng, fin, kt=kt):
                for k in range(NCOR):
                    fin(eng.dma_start(
                        out=e.a2av_i[k * TPC + kt * 128: k * TPC + (kt + 1) * 128, :],
                        in_=stg_v[:, (kt % 2) * 512 + k * 64:(kt % 2) * 512 + (k + 1) * 64]))
            tok_vst[kt] = pb.step("y", fvs, after=[tok_vev], n_dma=NCOR)

        def stq(eng, fin):
            for d in range(DCH):
                fin(eng.dma_start(out=e.a2aq_i[d * 128:(d + 1) * 128, :],
                                  in_=qT[:, d * TPC:(d + 1) * TPC]))
        tok_stq = pb.step("y", stq, after=[tok_rope[(sgg, uu)] for sgg in range(16)
                                           for uu in range(2)], n_dma=DCH)

        def stk(eng, fin):
            for d in range(4):
                fin(eng.dma_start(out=e.a2ak_i[d * 128:(d + 1) * 128, :],
                                  in_=kT[:, d * TPC:(d + 1) * TPC]))
        tok_stk = pb.step("y", stk, after=[tok_rope[(sgg, uu)] for sgg in (16, 17, 18, 19)
                                           for uu in range(2)], n_dma=4)

        def collq(eng, fin):
            fin(eng.collective_compute("AllToAll", OP.bypass,
                                       replica_groups=[list(range(NCOR))],
                                       ins=[e.a2aq_i], outs=[e.a2aq_o]))
        tok_ccq = pb.step("g", collq, after=[tok_stq], cc=True)

        def collk(eng, fin):
            fin(eng.collective_compute("AllToAll", OP.bypass,
                                       replica_groups=[list(range(NCOR))],
                                       ins=[e.a2ak_i], outs=[e.a2ak_o]))
        tok_cck = pb.step("g", collk, after=[tok_stk], cc=True)

        def collv(eng, fin):
            fin(eng.collective_compute("AllToAll", OP.bypass,
                                       replica_groups=[list(range(NCOR))],
                                       ins=[e.a2av_i], outs=[e.a2av_o]))
        tok_ccv = pb.step("g", collv, after=[tok_vst[kk] for kk in range(8)], cc=True)
        e.tok_hfree = tok_vp
        e.tok_psum_free = [tok_rope[(NSG - 1, 1)], prev_vev]
        pb.flush()

    with ExitStack() as a2ctx:
        en = a2ctx.enter_context
        kcache = en(nc.sbuf_tensor("a_kc", [128, S], BF16))
        vfull = en(nc.sbuf_tensor("a_vf", [128, 64 * 65], BF16))
        qtiles = en(nc.sbuf_tensor("a_qt", [128, 4 * 512], BF16))
        pring = en(nc.sbuf_tensor("a_pr", [128, 6 * 1024], BF16))
        attnT = en(nc.sbuf_tensor("a_att", [64, 4 * 1024], BF16))
        att_f32 = en(nc.sbuf_tensor("a_af", [65, 1024], F32))
        rec_sc = en(nc.sbuf_tensor("a_rec", [128, 1024], F32))
        S0 = en(nc.psum_tensor("a_S0", [128, 1024], F32))
        S1 = en(nc.psum_tensor("a_S1", [128, 1024], F32))
        O0 = en(nc.psum_tensor("a_O0", [65, 512], F32))
        O1 = en(nc.psum_tensor("a_O1", [65, 512], F32))
        p_rb = en(nc.psum_tensor("a_rb", [64, 1024], F32))
        Sps = [S0, S1]
        NKT = [4 * (qc + 1) for qc in range(16)]

        def kcl(eng, fin):
            for half in range(2):
                for j in range(NCOR):
                    fin(eng.dma_start(out=kcache[half * 64:(half + 1) * 64, j * TPC:(j + 1) * TPC],
                                      in_=e.a2ak_o[j * 64:(j + 1) * 64, :]))
        tok_kc = pb.step("y", kcl, after=[tok_cck, tok_ccq, tok_ccv], n_dma=16)

        def vfl(eng, fin):
            for qq in range(4):
                src = e.a2av_o.rearrange("(kt p) j -> p kt j", p=128)[:, qq * 16:(qq + 1) * 16, :]
                dst = vfull[:].rearrange("p (kt j) -> p kt j", kt=64)[:, qq * 16:(qq + 1) * 16, 0:64]
                fin(eng.dma_start(out=dst, in_=src))
        tok_vf = pb.step("y", vfl, after=[tok_ccv, tok_ccq, tok_cck], n_dma=4)

        def vone(eng, fin):
            fin(eng.memset(vfull[:].rearrange("p (kt j) -> p kt j", kt=64)[:, :, 64:65], 1.0))
        tok_vones = pb.step("v", vone, after=[tok_ccq, tok_cck, tok_ccv])

        tok_qt = {}
        tok_S = {}
        tok_exp = {}
        tok_msk = {}
        tok_AV = {}
        tok_oev = {}
        tok_norm = {}
        tok_sto = {}
        prev_rc = None
        nkt_all = 0
        pend_av = [None]
        av_uid = [0]

        def kts_of(qcv, pair):
            return 0

        for qc in range(16):
            for pair in range(2):
                u = qc * 2 + pair

                def fqt(eng, fin, qc=qc, pair=pair, u=u):
                    fin(eng.dma_start(
                        out=qtiles[:, (u % 4) * 512:(u % 4 + 1) * 512],
                        in_=e.a2aq_o[(qc // 2) * 256 + pair * 128:(qc // 2) * 256 + (pair + 1) * 128,
                                     (qc % 2) * 512:((qc % 2) + 1) * 512]))
                deps = [tok_ccq, tok_cck, tok_ccv]
                if u >= 4:
                    u4 = u - 4
                    deps.append(tok_S[(u4, NKT[u4 // 2] - 1)])
                tok_qt[u] = pb.step("y", fqt, after=deps, n_dma=1)

                def emit_av(qcv, ktv, nv):
                    def fav(eng, fin, qc=qcv, kt=ktv, nkt_all=nv):
                        P = pring[:, (nkt_all % 6) * 1024:(nkt_all % 6 + 1) * 1024]
                        vt = vfull[:, kt * 65:(kt + 1) * 65]
                        eng.matmul(O0[:], vt, P[:, 0:512],
                                   start=(kt == 0), stop=(kt == NKT[qcv] - 1))
                        fin(eng.matmul(O1[:], vt, P[:, 512:1024],
                                       start=(kt == 0), stop=(kt == NKT[qcv] - 1)))
                    diagv = ktv >= NKT[qcv] - 4
                    deps = [tok_msk[nv] if diagv else tok_exp[nv], tok_vf, tok_vones]
                    if nv == 0:
                        deps += e.tok_psum_free
                    if ktv == 0 and av_uid[0] > 0:
                        deps.append(tok_oev[av_uid[0] - 1])
                    tok_AV[nv] = pb.step("t", fav, after=deps)

                for kt in range(NKT[qc]):
                    def fS(eng, fin, u=u, kt=kt, nkt_all=nkt_all):
                        Sp = Sps[nkt_all % 2]
                        qt = qtiles[:, (u % 4) * 512:(u % 4 + 1) * 512]
                        eng.matmul(Sp[:, 0:512], kcache[0:64, kt * 128:(kt + 1) * 128],
                                   qt[0:64, :], start=True, stop=True, tile_position=(0, 0))
                        fin(eng.matmul(Sp[:, 512:1024], kcache[64:128, kt * 128:(kt + 1) * 128],
                                       qt[64:128, :], start=True, stop=True,
                                       tile_position=(64, 0)))
                    deps = [tok_qt[u], tok_kc]
                    if nkt_all < 2:
                        deps += e.tok_psum_free
                    if nkt_all >= 2:
                        deps.append(tok_exp[nkt_all - 2])
                    tok_S[(u, kt)] = pb.step("t", fS, after=deps)

                    def fexp(eng, fin, nkt_all=nkt_all):
                        Sp = Sps[nkt_all % 2]
                        P = pring[:, (nkt_all % 6) * 1024:(nkt_all % 6 + 1) * 1024]
                        fin(eng.activation(P, Sp[:], AF.Exp, scale=0.125))
                    deps = [tok_S[(u, kt)]]
                    if nkt_all >= 6:
                        deps.append(tok_AV[nkt_all - 6])
                    tok_exp[nkt_all] = pb.step("s", fexp, after=deps)

                    diag = kt >= NKT[qc] - 4
                    if diag:
                        def fmsk(eng, fin, qc=qc, kt=kt, nkt_all=nkt_all):
                            P = pring[:, (nkt_all % 6) * 1024:(nkt_all % 6 + 1) * 1024]
                            base = qc * 512 - kt * 128
                            eng.affine_select(P[:, 0:512], P[:, 0:512], pattern=[[1, 512]],
                                              compare_op=OP.is_ge, fill=0.0, base=base,
                                              channel_multiplier=-1)
                            fin(eng.affine_select(P[:, 512:1024], P[:, 512:1024],
                                                  pattern=[[1, 512]], compare_op=OP.is_ge,
                                                  fill=0.0, base=base, channel_multiplier=-1))
                        tok_msk[nkt_all] = pb.step("g", fmsk, after=[tok_exp[nkt_all]])

                    if pend_av[0] is not None:
                        emit_av(*pend_av[0])
                    pend_av[0] = (qc, kt, nkt_all)
                    nkt_all += 1
                emit_av(*pend_av[0])
                pend_av[0] = None
                av_uid[0] = u + 1

                def foe(eng, fin):
                    eng.tensor_copy(att_f32[0:65, 0:512], O0[:])
                    fin(eng.tensor_copy(att_f32[0:65, 512:1024], O1[:]))
                deps = [tok_AV[nkt_all - 1]]
                if u > 0:
                    deps.append(tok_norm[u - 1])
                tok_oev[u] = pb.step("v", foe, after=deps)

                def frb(eng, fin):
                    eng.matmul(p_rb[:, 0:512], e.ones_row[64:65, 0:64],
                               att_f32[64:65, 0:512], start=True, stop=True)
                    fin(eng.matmul(p_rb[:, 512:1024], e.ones_row[64:65, 0:64],
                                   att_f32[64:65, 512:1024], start=True, stop=True))
                bdeps = [tok_oev[u]]
                if u == 0:
                    bdeps += e.tok_psum_free
                if prev_rc is not None:
                    bdeps.append(prev_rc)
                tok_rb = pb.step("t", frb, after=bdeps)

                def frc(eng, fin):
                    fin(eng.reciprocal(rec_sc[0:64, 0:1024], p_rb[:]))
                tok_rc = pb.step("v", frc, after=[tok_rb])
                prev_rc = tok_rc

                def fnm(eng, fin, qc=qc, pair=pair):
                    c0 = (qc % 2) * 512
                    eng.tensor_tensor(attnT[0:64, (2 * pair) * 1024 + c0:(2 * pair) * 1024 + c0 + 512],
                                      att_f32[0:64, 0:512], rec_sc[0:64, 0:512], OP.mult)
                    fin(eng.tensor_tensor(attnT[0:64, (2 * pair + 1) * 1024 + c0:(2 * pair + 1) * 1024 + c0 + 512],
                                          att_f32[0:64, 512:1024], rec_sc[0:64, 512:1024], OP.mult))
                ndeps = [tok_rb]
                if qc % 2 == 0 and qc >= 2:
                    ndeps.append(tok_sto[qc // 2 - 1])
                tok_norm[u] = pb.step("v", fnm, after=ndeps)

                if pair == 1 and qc % 2 == 1:
                    def fsto(eng, fin, qc=qc):
                        k = qc // 2
                        for h in range(4):
                            fin(eng.dma_start(
                                out=e.a2ao_i[256 * k + h * 64: 256 * k + (h + 1) * 64, :],
                                in_=attnT[0:64, h * 1024:(h + 1) * 1024]))
                    tok_sto[qc // 2] = pb.step("y", fsto,
                                               after=[tok_norm[qc * 2], tok_norm[qc * 2 + 1]],
                                               n_dma=4)

        def collo(eng, fin):
            fin(eng.collective_compute("AllToAll", OP.bypass,
                                       replica_groups=[list(range(NCOR))],
                                       ins=[e.a2ao_i], outs=[e.a2ao_o]))
        tok_cco = pb.step("g", collo, after=[tok_sto[NCOR - 1]], cc=True)
        e.tok_psum_free = [tok_norm[31], tok_oev[31]]
        pb.flush()

    with ExitStack() as a3ctx:
        en = a3ctx.enter_context
        attnq = en(nc.sbuf_tensor("a_aq", [128, DCH * TPC], BF16))
        wo0 = en(nc.sbuf_tensor("a_wo0", [128, 16 * 512], BF16))
        wo1 = en(nc.sbuf_tensor("a_wo1", [128, 16 * 512], BF16))
        po0 = en(nc.psum_tensor("a_po0", [128, 512], F32))
        po1 = en(nc.psum_tensor("a_po1", [128, 512], F32))
        wobufs = [wo0, wo1]
        pos_ = [po0, po1]

        def aql(eng, fin):
            for d in range(DCH):
                fin(eng.dma_start(out=attnq[:, d * TPC:(d + 1) * TPC],
                                  in_=e.a2ao_o[d * 128:(d + 1) * 128, :]))
        tok_aq = pb.step("y", aql, after=[tok_cco], n_dma=DCH)

        tok_wog = {}
        tok_womm = {}
        tok_xu = {}

        def wo_load(g):
            def fwo(eng, fin, g=g):
                fin(eng.dma_start(out=wobufs[g % 2][:],
                                  in_=e.woa_t[g * 128:(g + 1) * 128, :]))
            deps = []
            if g < 2:
                deps += e.tok_psum_free
                deps.append(tok_cco)
            if g >= 2:
                deps.append(tok_womm[(g - 2, 3, 1)])
            tok_wog[g] = pb.step("g", fwo, after=deps, n_dma=1)

        wo_load(0)
        wo_load(1)
        for g in range(4):
            for dc in range(4):
                for th in range(2):
                    uu = dc * 2 + th
                    lin = g * 8 + uu

                    def fwm(eng, fin, g=g, dc=dc, th=th, uu=uu):
                        po = pos_[uu % 2]
                        ins = None
                        for d in range(DCH):
                            ins = eng.matmul(po[:], wobufs[g % 2][:, d * 512 + dc * 128: d * 512 + (dc + 1) * 128],
                                             attnq[:, d * TPC + th * 512: d * TPC + (th + 1) * 512],
                                             start=(d == 0), stop=(d == DCH - 1))
                        fin(ins)
                    deps = [tok_wog[g], tok_aq]
                    if lin < 2:
                        deps += e.tok_psum_free
                    if lin >= 2:
                        pl = lin - 2
                        deps.append(tok_xu[pl])
                    tok_womm[(g, dc, th)] = pb.step("t", fwm, after=deps)

                    def fxu(eng, fin, g=g, dc=dc, th=th, uu=uu):
                        gd = g * 4 + dc
                        xs = e.xT[:, gd * TPC + th * 512: gd * TPC + (th + 1) * 512]
                        fin(eng.tensor_tensor(xs, xs, pos_[uu % 2][:], OP.add))
                    tok_xu[lin] = pb.step("v", fxu, after=[tok_womm[(g, dc, th)], e.tok_x])
            if g + 2 < 4:
                wo_load(g + 2)
        e.tok_x = tok_xu[31]
        e.tok_psum_free = [e.tok_x]
        pb.flush()


def emit_attn_b(e):
    pb, nc = e.pb, e.nc
    SC_B = float(1.0 / np.sqrt(128.0))
    abctx = ExitStack()
    kb1 = abctx.enter_context(nc.sbuf_tensor("b_k1", [128, 32 * 128], BF16))
    kb2 = abctx.enter_context(nc.sbuf_tensor("b_k2", [128, 32 * 128], BF16))
    qd1 = abctx.enter_context(nc.sbuf_tensor("b_qd1", [128, 1024], BF16))
    qd2 = abctx.enter_context(nc.sbuf_tensor("b_qd2", [128, 1024], BF16))
    with ExitStack() as b1ctx:
        en = b1ctx.enter_context
        w0 = en(nc.sbuf_tensor("b_w0", [128, 16 * 512], BF16))
        w1 = en(nc.sbuf_tensor("b_w1", [128, 16 * 512], BF16))
        stg = en(nc.sbuf_tensor("b_st", [128, 4 * 512], BF16))
        vloc = en(nc.sbuf_tensor("b_vl", [128, 8 * 2048], BF16))
        pq0 = en(nc.psum_tensor("b_pq0", [128, 512], F32))
        pq1 = en(nc.psum_tensor("b_pq1", [128, 512], F32))
        ps_lam = en(nc.psum_tensor("b_lam", [128, 1], F32))
        wbufs = [w0, w1]
        pqs = [pq0, pq1]
        tok_w = {}
        tok_pj = {}
        tok_ev = {}
        tok_st = {}
        grp_src = [(e.wkb_t, i) for i in range(4)] + \
                  [(e.wqb_t, i) for i in range(4)] + \
                  [(e.wvb_t, i) for i in range(4)]

        def wload(gi):
            src, g = grp_src[gi]

            def fw(eng, fin, src=src, g=g, gi=gi):
                fin(eng.dma_start(out=wbufs[gi % 2][:],
                                  in_=src[g * 128:(g + 1) * 128, :]))
            deps = []
            if gi < 2:
                deps.append(e.tok_h)
            if gi >= 2:
                deps.append(tok_pj[(gi - 2, 7)])
            tok_w[gi] = pb.step("g", fw, after=deps, n_dma=1)

        tok_cc = {}

        def emit_agk():
            def agk(eng, fin):
                fin(eng.collective_compute("AllGather", OP.bypass,
                                           replica_groups=[list(range(NCOR))],
                                           ins=[e.agk_i], outs=[e.agk_o]))
            tok_cc["agk"] = pb.step("g", agk, after=[tok_st[(gg, uu)] for gg in range(4)
                                                     for uu in range(8)], cc=True)

        def emit_agvA():
            def agvA(eng, fin):
                fin(eng.collective_compute("AllGather", OP.bypass,
                                           replica_groups=[list(range(NCOR))],
                                           ins=[e.agv_iA], outs=[e.agv_oA]))
            tok_cc["agvA"] = pb.step("g", agvA, after=[tok_stvA[0]], cc=True)

        tok_stvA = [None]
        wload(0)
        wload(1)
        for gi in range(12):
            isv = gi >= 8
            for ch in range(4):
                for th in range(2):
                    u = ch * 2 + th
                    lin = gi * 8 + u

                    def fp(eng, fin, gi=gi, ch=ch, th=th, u=u, isv=isv):
                        pq = pqs[u % 2]
                        ins = None
                        for d in range(DCH):
                            if isv:
                                kt = ch * 2 + th
                                ins = eng.matmul(pq[:], e.hT[:, d * TPC + kt * 128: d * TPC + (kt + 1) * 128],
                                                 wbufs[gi % 2][:, d * 512:(d + 1) * 512],
                                                 start=(d == 0), stop=(d == DCH - 1))
                            else:
                                ins = eng.matmul(pq[:], wbufs[gi % 2][:, d * 512 + ch * 128: d * 512 + (ch + 1) * 128],
                                                 e.hT[:, d * TPC + th * 512: d * TPC + (th + 1) * 512],
                                                 start=(d == 0), stop=(d == DCH - 1))
                        fin(ins)
                    deps = [tok_w[gi], e.tok_h]
                    if gi == 0 and u < 2:
                        deps += e.tok_psum_free
                    prev = (gi, u - 2) if u >= 2 else ((gi - 1, 6 + u) if gi > 0 else None)
                    if prev in tok_st:
                        deps.append(tok_st[prev])
                    tok_pj[(gi, u)] = pb.step("t", fp, after=deps)

                    def fev(eng, fin, gi=gi, ch=ch, th=th, u=u, isv=isv):
                        if isv:
                            kt = ch * 2 + th
                            g = gi - 8
                            fin(eng.activation(vloc[:, kt * 2048 + g * 512: kt * 2048 + (g + 1) * 512],
                                               pqs[u % 2][:], AF.Copy))
                        else:
                            fin(eng.activation(stg[:, (u % 4) * 512:(u % 4 + 1) * 512],
                                               pqs[u % 2][:], AF.Copy))
                    edeps = [tok_pj[(gi, u)]]
                    if not isv and lin >= 4:
                        pl = lin - 4
                        edeps.append(tok_st[(pl // 8, pl % 8)])
                    tok_ev[(gi, u)] = pb.step("s", fev, after=edeps)

                    if not isv:
                        def fst(eng, fin, gi=gi, ch=ch, th=th, u=u):
                            row = ((gi % 4) * 4 + ch) * 128
                            dram = e.agk_i if gi < 4 else e.qb_dram
                            fin(eng.dma_start(out=dram[row:row + 128, th * 512:(th + 1) * 512],
                                              in_=stg[:, (u % 4) * 512:(u % 4 + 1) * 512]))
                        tok_st[(gi, u)] = pb.step("y", fst, after=[tok_ev[(gi, u)]], n_dma=1)
                    else:
                        tok_st[(gi, u)] = tok_ev[(gi, u)]
            if gi + 2 < 12:
                wload(gi + 2)
            if gi == 3:
                emit_agk()
            if gi == 9:
                def stvA(eng, fin):
                    for kt in range(8):
                        fin(eng.dma_start(out=e.agv_iA[kt * 128:(kt + 1) * 128, :],
                                          in_=vloc[:, kt * 2048: kt * 2048 + 1024]))
                tok_stvA[0] = pb.step("y", stvA, after=[tok_ev[(gii, uu)] for gii in (8, 9)
                                                        for uu in range(8)], n_dma=8)
                emit_agvA()

        def stvB(eng, fin):
            for kt in range(8):
                fin(eng.dma_start(out=e.agv_iB[kt * 128:(kt + 1) * 128, :],
                                  in_=vloc[:, kt * 2048 + 1024:(kt + 1) * 2048]))
        tok_stvB = pb.step("y", stvB, after=[tok_ev[(gii, uu)] for gii in (10, 11)
                                             for uu in range(8)], n_dma=8)

        tok_agk = tok_cc["agk"]
        tok_agvA = tok_cc["agvA"]

        def agvB(eng, fin):
            fin(eng.collective_compute("AllGather", OP.bypass,
                                       replica_groups=[list(range(NCOR))],
                                       ins=[e.agv_iB], outs=[e.agv_oB]))
        tok_agvB = pb.step("g", agvB, after=[tok_stvB], cc=True)

        def flam1a(eng, fin):
            eng.tensor_tensor(e.stat[0:1, 512:640], e.stat[0:1, 0:128],
                              e.stat[0:1, 128:256], OP.mult)
            fin(eng.tensor_tensor(e.stat[0:1, 640:768], e.stat[0:1, 256:384],
                                  e.stat[0:1, 384:512], OP.mult))
        tok_l1a = pb.step("v", flam1a, after=[e.tok_lam_in, e.tok_rope])

        def flam1(eng, fin):
            eng.reduce_sum(e.stat[0:1, 768:769], e.stat[0:1, 512:640], axis=AX.X)
            fin(eng.reduce_sum(e.stat[0:1, 769:770], e.stat[0:1, 640:768], axis=AX.X))
        tok_l1 = pb.step("v", flam1, after=[tok_l1a])

        def flam2(eng, fin):
            fin(eng.activation(e.stat[0:1, 770:772], e.stat[0:1, 768:770], AF.Exp))
        tok_l2 = pb.step("s", flam2, after=[tok_l1])

        tok_l3a = pb.step("v", lambda g_, fin: fin(g_.tensor_tensor(
            e.stat[0:1, 772:773], e.stat[0:1, 770:771],
            e.stat[0:1, 771:772], OP.subtract)), after=[tok_l2])
        tok_l3 = pb.step("v", lambda g_, fin: fin(g_.tensor_scalar(
            e.stat[0:1, 772:773], e.stat[0:1, 772:773],
            0.8, None, OP.add)), after=[tok_l3a])

        def flam4(eng, fin):
            fin(eng.matmul(ps_lam[:], e.ones_row[0:1, 0:128], e.stat[0:1, 772:773],
                           start=True, stop=True))
        tok_l4 = pb.step("t", flam4, after=[tok_l3, e.tok_ones] + e.tok_psum_free)

        def flam5(eng, fin):
            fin(eng.tensor_scalar(e.neg_lam[:], ps_lam[:], -1.0, None, OP.mult))
        e.tok_lam = pb.step("v", flam5, after=[tok_l4])
        e.tok_hfree = tok_pj[(11, 7)]
        tok_qstaged = tok_st[(7, 7)]
        e.tok_psum_free = [tok_ev[(11, 7)], e.tok_lam]
        pb.flush()

    with ExitStack() as b2ctx:
        en = b2ctx.enter_context
        vbq = {}
        for vh in range(2):
            for par in range(2):
                vbq[(vh, par)] = en(nc.sbuf_tensor(f"b_v{vh}{par}", [128, 32 * 129], BF16))
        pring = en(nc.sbuf_tensor("b_pr", [128, 5 * 1024], BF16))
        o1sb = en(nc.sbuf_tensor("b_o1", [128, 2 * 4 * 129], F32))
        o2sb = en(nc.sbuf_tensor("b_o2", [128, 2 * 4 * 129], F32))
        normed = en(nc.sbuf_tensor("b_nm", [128, 8 * 2048], BF16))
        cscr = en(nc.sbuf_tensor("b_cs", [128, 4 * 128], F32))
        cscr2 = en(nc.sbuf_tensor("b_cs2", [128, 4 * 128], F32))
        cs3 = en(nc.sbuf_tensor("b_cs3", [128, 32], F32))
        S0 = en(nc.psum_tensor("b_S0", [128, 1024], F32))
        S1 = en(nc.psum_tensor("b_S1", [128, 1024], F32))
        Oa = en(nc.psum_tensor("b_Oa", [128, 129], F32))
        Ob = en(nc.psum_tensor("b_Ob", [128, 129], F32))
        Oc = en(nc.psum_tensor("b_Oc", [128, 129], F32))
        Od = en(nc.psum_tensor("b_Od", [128, 129], F32))
        Sps = [S0, S1]
        Ops_ = [Oa, Ob, Oc, Od]
        kbufs = {1: kb1, 2: kb2}

        def vmemset(eng, fin):
            ins = None
            for b in vbq.values():
                ins = eng.memset(b[:].rearrange("p (u j) -> p u j", u=32)[:, :, 128:129], 1.0)
            fin(ins)
        tok_vone = pb.step("v", vmemset, after=[tok_agvB])

        tok_kld = {}
        tok_qd = {}
        tok_vld = {}
        tok_S = {}
        tok_exp = {}
        tok_AV = {}
        tok_o1ev = {}
        tok_o2ev = {}
        tok_cmb = {}
        ns2key = {}
        prev_o2ev = None
        prev_c2 = [None]
        pend_chain = [None]
        nS = 0
        pend_avb = [None]
        def vload(hh, vh):
            def fvl(eng, fin, hh=hh, vh=vh):
                agv = e.agv_oA if hh < 8 else e.agv_oB
                c0 = (hh % 8) * 128
                for gg in range(2):
                    G = vh * 2 + gg
                    src = agv[G * 2048:(G + 1) * 2048, c0:c0 + 128] \
                        .rearrange("(kt p) c -> p kt c", p=128)
                    fin(eng.dma_start(
                        out=vbq[(vh, hh % 2)][:].rearrange("p (u j) -> p u j", u=32)
                        [:, gg * 16:(gg + 1) * 16, 0:128],
                        in_=src))
            deps = [tok_agvA if hh < 8 else tok_agvB, tok_vone]
            if hh >= 2:
                deps.append(tok_AV[(hh - 2, 2, 1, 31)])
            tok_vld[(hh, vh)] = pb.step("y", fvl, after=deps, n_dma=2)

        for h in range(16):
            m, par = h // 2, h % 2
            if h == 0:
                for hh in (0, 1):
                    vload(hh, 0)
                    vload(hh, 1)
            elif h + 1 < 16:
                vload(h + 1, 0)
                vload(h + 1, 1)
            for half in (1, 2):
                def fkl(eng, fin, m=m, par=par, half=half):
                    base = m * 256 + (half - 1) * 128 + par * 64
                    kv = e.agk_o.rearrange("(r f) t -> f r t", r=NCOR)
                    for lo in range(2):
                        fin(eng.dma_start(
                            out=kbufs[half][lo * 64:(lo + 1) * 64, :]
                            .rearrange("f (r t) -> f r t", r=4),
                            in_=kv[base:base + 64, lo * 4:(lo + 1) * 4, :]))
                kdeps = [tok_agk]
                if (h - 1, half) in tok_kld:
                    kdeps.append(tok_S[(h - 1, half, 1, 31)])
                tok_kld[(h, half)] = pb.step("y", fkl, after=kdeps, n_dma=2)

                def fqd(eng, fin, m=m, par=par, half=half):
                    row = m * 256 + (half - 1) * 128 + par * 64
                    qd = qd1 if half == 1 else qd2
                    fin(eng.dma_start(out=qd[0:64, :], in_=e.qb_dram[row:row + 64, :]))
                    fin(eng.dma_start(out=qd[64:128, :], in_=e.qb_dram[row:row + 64, :]))
                qdeps = [tok_qstaged]
                if (h - 1, half) in tok_qd:
                    qdeps.append(tok_S[(h - 1, half, 1, 31)])
                tok_qd[(h, half)] = pb.step("y", fqd, after=qdeps, n_dma=2)

            for qc in range(2):
                for half in (1, 2):

                    def emit_avb(hv, halfv, qcv, tpv, nv):
                        def fav(eng, fin, hv=hv, tp=tpv, nS=nv):
                            P = pring[:, (nS % 5) * 1024:(nS % 5 + 1) * 1024]
                            ins = None
                            for sub in range(2):
                                kt = tp * 2 + sub
                                vt = vbq[(sub, hv % 2)][:, tp * 129:(tp + 1) * 129]
                                for qs in range(4):
                                    ins = eng.matmul(
                                        Ops_[qs][:],
                                        P[:, sub * 512 + qs * 128: sub * 512 + (qs + 1) * 128],
                                        vt, start=(kt == 0), stop=(kt == 63))
                            fin(ins)
                        adeps = [tok_exp[nv]]
                        if tpv == 0:
                            adeps += [tok_vld[(hv, 0)], tok_vld[(hv, 1)]]
                        if nv == 0:
                            adeps += e.tok_psum_free
                        if tpv == 0:
                            if halfv == 2:
                                adeps.append(tok_o1ev[(hv, qcv)])
                            elif prev_o2ev is not None:
                                adeps.append(prev_o2ev)
                        tok_AV[(hv, halfv, qcv, tpv)] = pb.step("t", fav, after=adeps)

                    for tp in range(32):

                        def fS(eng, fin, half=half, qc=qc, tp=tp, nS=nS):
                            Sp = Sps[nS % 2]
                            kb = kbufs[half]
                            qd_ = qd1 if half == 1 else qd2
                            eng.matmul(Sp[:, 0:512], kb[0:64, tp * 128:(tp + 1) * 128],
                                       qd_[0:64, qc * 512:(qc + 1) * 512],
                                       start=True, stop=True, tile_position=(0, 0))
                            fin(eng.matmul(Sp[:, 512:1024], kb[64:128, tp * 128:(tp + 1) * 128],
                                           qd_[64:128, qc * 512:(qc + 1) * 512],
                                           start=True, stop=True, tile_position=(64, 0)))
                        deps = [tok_kld[(h, half)], tok_qd[(h, half)]]
                        if nS < 2:
                            deps += e.tok_psum_free
                        if nS >= 2:
                            deps.append(tok_exp[nS - 2])
                        tok_S[(h, half, qc, tp)] = pb.step("t", fS, after=deps)

                        def fexp(eng, fin, nS=nS):
                            Sp = Sps[nS % 2]
                            P = pring[:, (nS % 5) * 1024:(nS % 5 + 1) * 1024]
                            fin(eng.activation(P, Sp[:], AF.Exp, scale=SC_B))
                        edeps = [tok_S[(h, half, qc, tp)]]
                        if nS >= 5:
                            edeps.append(tok_AV[ns2key[nS - 5]])
                        tok_exp[nS] = pb.step("s", fexp, after=edeps)
                        ns2key[nS] = (h, half, qc, tp)

                        if pend_avb[0] is not None:
                            emit_avb(*pend_avb[0])
                        pend_avb[0] = (h, half, qc, tp, nS)
                        nS += 1
                    emit_avb(*pend_avb[0])
                    pend_avb[0] = None

                    pidx = h * 2 + qc
                    pb_ = (pidx % 2)
                    ob1 = o1sb[:, pb_ * 516:(pb_ + 1) * 516]
                    ob2 = o2sb[:, pb_ * 516:(pb_ + 1) * 516]
                    cs = cs3[:, pb_ * 16:(pb_ + 1) * 16]
                    if half == 1:
                        def fo1(eng, fin, ob1=ob1):
                            ins = None
                            for qs in range(4):
                                ins = eng.tensor_copy(ob1[:, qs * 129:(qs + 1) * 129],
                                                      Ops_[qs][:])
                            fin(ins)
                        tok_o1ev[(h, qc)] = pb.step("v", fo1,
                                                    after=[tok_AV[(h, 1, qc, 31)]])
                    else:
                        def fo2(eng, fin, ob2=ob2):
                            ins = None
                            for qs in range(4):
                                ins = eng.tensor_copy(ob2[:, qs * 129:(qs + 1) * 129],
                                                      Ops_[qs][:])
                            fin(ins)
                        tok_o2ev[(h, qc)] = pb.step("v", fo2,
                                                    after=[tok_AV[(h, 2, qc, 31)]])
                        prev_o2ev = tok_o2ev[(h, qc)]

                        def emit_chain(h=h, qc=qc, ob1=ob1, ob2=ob2, cs=cs):
                            def fc1a(eng, fin):
                                ins = None
                                for qs in range(4):
                                    eng.reciprocal(cs[:, qs * 4:qs * 4 + 1],
                                                   ob1[:, qs * 129 + 128: qs * 129 + 129])
                                    ins = eng.reciprocal(cs[:, qs * 4 + 1:qs * 4 + 2],
                                                         ob2[:, qs * 129 + 128: qs * 129 + 129])
                                fin(ins)
                            tok_c1a = pb.step("v", fc1a,
                                              after=[tok_o2ev[(h, qc)], e.tok_lam])

                            def fc1b(eng, fin):
                                ins = None
                                for qs in range(4):
                                    ins = eng.tensor_tensor(cs[:, qs * 4 + 2:qs * 4 + 3],
                                                            cs[:, qs * 4 + 1:qs * 4 + 2],
                                                            e.neg_lam[:], OP.mult)
                                fin(ins)
                            tok_c1b = pb.step("g", fc1b, after=[tok_c1a])

                            def fc1c(eng, fin):
                                ins = None
                                for qs in range(4):
                                    ins = eng.tensor_scalar(cscr2[:, qs * 128:(qs + 1) * 128],
                                                            ob1[:, qs * 129: qs * 129 + 128],
                                                            cs[:, qs * 4:qs * 4 + 1], None, OP.mult)
                                fin(ins)
                            c1cdeps = [tok_c1b]
                            if prev_c2[0] is not None:
                                c1cdeps.append(prev_c2[0])
                            tok_c1c = pb.step("v", fc1c, after=c1cdeps)

                            def fc1d(eng, fin):
                                ins = None
                                for qs in range(4):
                                    ins = eng.scalar_tensor_tensor(
                                        cscr[:, qs * 128:(qs + 1) * 128],
                                        ob2[:, qs * 129: qs * 129 + 128],
                                        cs[:, qs * 4 + 2:qs * 4 + 3],
                                        cscr2[:, qs * 128:(qs + 1) * 128], OP.mult, OP.add)
                                fin(ins)
                            tok_c1 = pb.step("v", fc1d, after=[tok_c1c])

                            def fc2(eng, fin):
                                ins = None
                                for qs in range(4):
                                    ins = eng.tensor_tensor(cscr2[:, qs * 128:(qs + 1) * 128],
                                                            cscr[:, qs * 128:(qs + 1) * 128],
                                                            cscr[:, qs * 128:(qs + 1) * 128],
                                                            OP.mult)
                                fin(ins)
                            tok_c2g = pb.step("g", fc2, after=[tok_c1])

                            def fc2r(eng, fin):
                                ins = None
                                for qs in range(4):
                                    ins = eng.reduce_sum(ob1[:, qs * 129 + 128: qs * 129 + 129],
                                                         cscr2[:, qs * 128:(qs + 1) * 128],
                                                         axis=AX.X)
                                fin(ins)
                            tok_c2 = pb.step("v", fc2r, after=[tok_c2g])
                            prev_c2[0] = tok_c2

                            def fc3(eng, fin):
                                ins = None
                                for qs in range(4):
                                    ins = eng.tensor_scalar(ob1[:, qs * 129 + 128: qs * 129 + 129],
                                                            ob1[:, qs * 129 + 128: qs * 129 + 129],
                                                            float(1.0 / 128.0), RMS_EPS,
                                                            OP.mult, OP.add)
                                fin(ins)
                            tok_c3 = pb.step("g", fc3, after=[tok_c2])

                            def fc4a(eng, fin):
                                ins = None
                                for qs in range(4):
                                    ins = eng.activation(cs[:, qs * 4 + 3:qs * 4 + 4],
                                                         ob1[:, qs * 129 + 128: qs * 129 + 129],
                                                         AF.Ln)
                                fin(ins)
                            tok_c4a = pb.step("s", fc4a, after=[tok_c3])

                            def fc4(eng, fin):
                                ins = None
                                for qs in range(4):
                                    ins = eng.activation(cs[:, qs * 4 + 3:qs * 4 + 4],
                                                         cs[:, qs * 4 + 3:qs * 4 + 4], AF.Exp,
                                                         scale=-0.5, bias=e.cbias[:, 1:2])
                                fin(ins)
                            tok_c4 = pb.step("s", fc4, after=[tok_c4a])

                            def fc5(eng, fin):
                                ins = None
                                for qs in range(4):
                                    qt = qc * 4 + qs
                                    a0 = qt * 2048 + h
                                    outap = normed[:, a0: a0 + 16 * 127 + 1: 16]
                                    ins = eng.tensor_scalar(outap,
                                                            cscr[:, qs * 128:(qs + 1) * 128],
                                                            cs[:, qs * 4 + 3:qs * 4 + 4],
                                                            None, OP.mult)
                                fin(ins)
                            tok_cmb[(h, qc)] = pb.step("v", fc5, after=[tok_c4])

                        if pend_chain[0] is not None:
                            pend_chain[0]()
                        pend_chain[0] = emit_chain

        if pend_chain[0] is not None:
            pend_chain[0]()
            pend_chain[0] = None

        def stn(eng, fin):
            for k in range(NCOR):
                src = normed[:].rearrange("p (qt c) -> p qt c", qt=8)[:, :, 256 * k:256 * (k + 1)]
                dst = e.a2an_i.rearrange("(k qt p) c -> p qt k c", k=NCOR, qt=8)[:, :, k, :]
                fin(eng.dma_start(out=dst, in_=src))
        tok_stn = pb.step("y", stn, after=[tok_cmb[(hh, qq)] for hh in range(16)
                                           for qq in range(2)], n_dma=NCOR)

        def colln(eng, fin):
            fin(eng.collective_compute("AllToAll", OP.bypass,
                                       replica_groups=[list(range(NCOR))],
                                       ins=[e.a2an_i], outs=[e.a2an_o]))
        tok_ccn = pb.step("g", colln, after=[tok_stn], cc=True)
        e.tok_psum_free = [tok_cmb[(15, 1)], prev_o2ev]
        pb.flush()

    abctx.close()
    with ExitStack() as b3ctx:
        en = b3ctx.enter_context
        tTf = en(nc.sbuf_tensor("b_tt", [128, 8 * 16 * 128], BF16))
        wo0 = en(nc.sbuf_tensor("b_wo0", [128, 16 * 512], BF16))
        wo1 = en(nc.sbuf_tensor("b_wo1", [128, 16 * 512], BF16))
        py0 = en(nc.psum_tensor("b_py0", [128, 128], F32))
        py1 = en(nc.psum_tensor("b_py1", [128, 128], F32))
        wobufs = [wo0, wo1]
        pys = [py0, py1]

        def ttl(eng, fin):
            for ci in range(8):
                c, isub = ci // 2, ci % 2
                src = e.a2an_o[2048 * c: 2048 * (c + 1), isub * 128:(isub + 1) * 128] \
                    .rearrange("(jc p) i -> p jc i", p=128)
                dst = tTf[:].rearrange("p (ci jc i) -> p ci jc i", ci=8, jc=16)[:, ci]
                fin(eng.dma_start(out=dst, in_=src))
        tok_tt = pb.step("y", ttl, after=[tok_ccn], n_dma=8)

        tok_wog = {}
        tok_mm = {}
        tok_xu = {}

        def wo_load(g):
            def fwo(eng, fin, g=g):
                fin(eng.dma_start(out=wobufs[g % 2][:],
                                  in_=e.wob_t[g * 128:(g + 1) * 128, :]))
            deps = []
            if g < 2:
                deps += e.tok_psum_free
                deps.append(tok_ccn)
            if g >= 2:
                deps.append(tok_mm[(g - 2, 7, 3)])
            tok_wog[g] = pb.step("g", fwo, after=deps, n_dma=1)

        wo_load(0)
        wo_load(1)
        for g in range(4):
            for ci in range(8):
                for dc4 in range(4):
                    lin = (g * 8 + ci) * 4 + dc4

                    def fm(eng, fin, g=g, ci=ci, dc4=dc4, lin=lin):
                        py = pys[lin % 2]
                        ins = None
                        for jc in range(16):
                            ins = eng.matmul(
                                py[:], wobufs[g % 2][:, jc * 512 + dc4 * 128: jc * 512 + (dc4 + 1) * 128],
                                tTf[:, (ci * 16 + jc) * 128:(ci * 16 + jc + 1) * 128],
                                start=(jc == 0), stop=(jc == 15))
                        fin(ins)
                    deps = [tok_wog[g], tok_tt]
                    if lin < 2:
                        deps += e.tok_psum_free
                    if lin >= 2:
                        deps.append(tok_xu[lin - 2])
                    tok_mm[(g, ci, dc4)] = pb.step("t", fm, after=deps)

                    def fx(eng, fin, g=g, ci=ci, dc4=dc4, lin=lin):
                        c, isub = ci // 2, ci % 2
                        dc = g * 4 + dc4
                        base = dc * TPC + isub * 512 + c
                        xv = e.xT[:, base: base + 509]
                        xv = bass.AP(xv.tensor, xv.offset,
                                     [xv.ap[0], [64, 8], [4, 16]])
                        pv = pys[lin % 2][:].rearrange("p (a b) -> p a b", a=8)
                        fin(eng.tensor_tensor(xv, xv, pv, OP.add))
                    tok_xu[lin] = pb.step("v", fx, after=[tok_mm[(g, ci, dc4)], e.tok_x])
            if g + 2 < 4:
                wo_load(g + 2)
        e.tok_x = tok_xu[127]
        e.tok_psum_free = [e.tok_x]
        pb.flush()


def emit_hcopy(e):
    pb = e.pb

    def fc(eng, fin):
        ins = None
        for d in range(DCH):
            ins = eng.tensor_copy(e.xT[:, d * TPC:(d + 1) * TPC],
                                  e.hT[:, d * TPC:(d + 1) * TPC])
        fin(ins)
    e.tok_x = pb.step("v", fc, after=[e.tok_h, e.tok_x])
    pb.flush()


def emit_store(e):
    pb = e.pb
    dbg = os.environ.get("A_DBG", "")

    def st(eng, fin):
        for d in range(DCH):
            fin(eng.dma_start(out=e.xT_out[d * 128:(d + 1) * 128, :],
                              in_=e.xT[:, d * TPC:(d + 1) * TPC]))
    tok = pb.step("y", st, after=[e.tok_x], n_dma=DCH)

    if dbg:
        src_map = {"qo": e.a2aq_o, "ko": e.a2ak_o, "oi": e.a2ao_i, "oo": e.a2ao_o,
                   "vo": e.a2av_o.rearrange("(a b) c -> a (b c)", b=16),
                   "no": e.a2an_o.rearrange("(a b) c -> a (b c)", b=4) if "attn_b" in PHASES else None,
                   "ko2": e.agk_o.rearrange("(a b) c -> a (b c)", b=8) if "attn_b" in PHASES else None}
        if dbg == "tbl":
            def fdbg(eng, fin):
                fin(eng.dma_start(out=e.xT_out[0:128, :], in_=e.cosq[:]))
                fin(eng.dma_start(out=e.xT_out[128:256, :], in_=e.sinq[:]))
                fin(eng.dma_start(out=e.xT_out[256:384, 0:4], in_=e.cbias[:]))
            tok = pb.step("g", fdbg, after=[tok], n_dma=3)
        else:
            sap = src_map[dbg]

            def fdbg(eng, fin):
                r = sap.shape[0]
                fin(eng.dma_start(out=e.xT_out[0:r, :], in_=sap))
            tok = pb.step("g", fdbg, after=[tok], n_dma=1)

    def fwait(eng, fin):
        fin(eng.nop())
    pb.step("y", fwait, after=[tok])


def _t4(w, SG):
    return np.ascontiguousarray(
        np.asarray(w, np.float32).reshape(SG, 128, 16, 128)
        .transpose(0, 3, 2, 1).reshape(SG * 128, 2048))


def _t512(w):
    return np.ascontiguousarray(
        np.asarray(w, np.float32).reshape(4, 512, 16, 128)
        .transpose(0, 3, 2, 1).reshape(512, 8192))


def _prep_weights(inputs):
    wq = np.asarray(inputs["L0_attn_a_wq"], np.float32)
    wqs = wq.reshape(1024, 2, 2048)[:, ::-1, :].reshape(2048, 2048)
    wk = np.asarray(inputs["L0_attn_a_wk"], np.float32)
    wks = wk.reshape(256, 2, 2048)[:, ::-1, :].reshape(512, 2048)
    wv = np.asarray(inputs["L0_attn_a_wv"], np.float32)
    wva_t = np.ascontiguousarray(wv.reshape(512, 16, 128).transpose(2, 1, 0)
                                 .reshape(128, 8192))
    idx = np.arange(2048)
    m_, rem = idx // 256, idx % 256
    half, rem2 = rem // 128, rem % 128
    par, rr = rem2 // 64, rem2 % 64
    perm = (2 * m_ + par) * 128 + half * 64 + rr
    wg = np.asarray(inputs["L0_ffn_w_gate"])
    wu = np.asarray(inputs["L0_ffn_w_up"])
    wd = np.asarray(inputs["L0_ffn_w_down"])
    return {
        "wqa_t": _t4(wq, 16),
        "wqa_st": _t4(wqs, 16),
        "wka_t": _t4(wk, 4),
        "wka_st": _t4(wks, 4),
        "wva_t": wva_t,
        "woa_t": _t512(np.asarray(inputs["L0_attn_a_wo"], np.float32)),
        "wqb_t": _t512(np.asarray(inputs["L0_attn_b_wq"], np.float32)[perm]),
        "wkb_t": _t512(np.asarray(inputs["L0_attn_b_wk"], np.float32)[perm]),
        "wvb_t": _t512(np.asarray(inputs["L0_attn_b_wv"], np.float32)),
        "wob_t": _t512(np.asarray(inputs["L0_attn_b_wo"], np.float32)),
        "w_gate": np.ascontiguousarray(
            wg.reshape(16, 128, 32, 256).transpose(2, 1, 0, 3).reshape(4096, 4096)),
        "w_up": np.ascontiguousarray(
            wu.reshape(16, 128, 32, 256).transpose(2, 1, 0, 3).reshape(4096, 4096)),
        "w_down": np.ascontiguousarray(
            wd.reshape(64, 128, 16, 128).transpose(2, 1, 0, 3).reshape(2048, 8192)),
        "lam_q1": np.asarray(inputs["L0_attn_b_lambda_q1"]).reshape(1, 128),
        "lam_q2": np.asarray(inputs["L0_attn_b_lambda_q2"]).reshape(1, 128),
        "lam_k1": np.asarray(inputs["L0_attn_b_lambda_k1"]).reshape(1, 128),
        "lam_k2": np.asarray(inputs["L0_attn_b_lambda_k2"]).reshape(1, 128),
        "s_gate": np.asarray(inputs["L0_ffn_s_gate"]).reshape(64, 128),
        "s_up": np.asarray(inputs["L0_ffn_s_up"]).reshape(64, 128),
        "s_down": np.asarray(inputs["L0_ffn_s_down"]).reshape(16, 128),
    }


def kernel(**inputs):
    x = np.asarray(inputs["L0_attn_a_x"])
    shared = _prep_weights(inputs)
    in_maps = []
    for c in range(NCOR):
        xs = np.ascontiguousarray(x[0, c * TPC:(c + 1) * TPC, :].T)
        pos = np.arange(c * TPC, (c + 1) * TPC, dtype=np.float32).reshape(1, TPC)
        m = dict(shared)
        m["xT_in"] = xs
        m["pos_in"] = pos
        in_maps.append(m)

    nc = build_program()
    res = run_bass_kernel_spmd(nc, in_maps, core_ids=list(range(NCOR)))
    out = np.empty((1, S, D), dtype=np.float32)
    for c in range(NCOR):
        out[0, c * TPC:(c + 1) * TPC, :] = np.asarray(res.results[c]["xT_out"]).T
    return out


# version 14
# speedup vs baseline: 1.8224x; 1.0142x over previous
import os
import numpy as np
import ml_dtypes
import concourse.bass as bass
import concourse.mybir as mybir
from concourse.bass_utils import run_bass_kernel_spmd
from concourse.alu_op_type import AluOpType as OP
from contextlib import ExitStack

os.environ.setdefault("TRN_ALLOW_ICI_ALL_TO_ALL", "1")

F32 = mybir.dt.float32
BF16 = mybir.dt.bfloat16
I8 = mybir.dt.int8
AF = mybir.ActivationFunctionType
AX = mybir.AxisListType

NCOR = 8
S = 8192
D = 2048
TPC = S // NCOR
DCH = D // 128
INTER = 8192
EPS_LN = 1e-5
RMS_EPS = 1.1920928955078125e-07
MAGIC = 12582912.0
PI = float(np.pi)
PI_CLAMP = float(np.pi * (1.0 - 1e-6))

PHASES = tuple(os.environ.get("A_PHASES", "attn_a,ffn1,attn_b,ffn2").split(","))

ENGS = ("t", "s", "v", "g", "y")


class PB:

    def __init__(self, nc, ctx, bl):
        self.nc, self.bl = nc, bl
        self.sems = {E: [ctx.enter_context(nc.semaphore("sem_" + E)), 0] for E in ENGS}
        self.dsems = {E: [ctx.enter_context(nc.semaphore("dsem_" + E)), 0] for E in ENGS}
        self.ccsem = [ctx.enter_context(nc.semaphore("sem_cc")), 0]
        self.q = {E: [] for E in ENGS}

    def step(self, E, fn, after=(), n_dma=0, cc=False):
        sem = self.ccsem if cc else (self.dsems[E] if n_dma else self.sems[E])
        inc_total = 1 if (cc or not n_dma) else 16 * n_dma
        waits = {}
        for tok in after:
            if tok is None:
                continue
            semref, val = tok
            key = id(semref)
            if key not in waits or waits[key][1] < val:
                waits[key] = (semref, val)
        sem[1] += inc_total
        self.q[E].append((fn, list(waits.values()), sem, inc_total, n_dma, cc))
        return (sem, sem[1])

    def flush(self):
        secmap = {"t": self.bl.tensor, "s": self.bl.scalar, "v": self.bl.vector,
                  "g": self.bl.gpsimd, "y": self.bl.sync}
        for E in ENGS:
            steps = self.q[E]
            self.q[E] = []
            if not steps:
                continue

            def make(steps=steps, E=E):
                def body(eng):
                    for (fn, waits, sem, inc_total, n_dma, cc) in steps:
                        for (semref, val) in waits:
                            eng.wait_ge(semref[0], val)
                        got = {"n": 0}
                        defk = 16 if n_dma else 1

                        def fin(instr, kk=None, _sem=sem, _got=got, _defk=defk):
                            k = _defk if kk is None else kk
                            instr.then_inc(_sem[0], k)
                            _got["n"] += k

                        fn(eng, fin)
                        assert got["n"] == inc_total, \
                            f"eng {E}: inc mismatch {got['n']} != {inc_total}"
                return body

            secmap[E](make())


class Env:
    pass


def build_program():
    nc = bass.Bass("TRN2", target_bir_lowering=False, detect_race_conditions=False)
    e = Env()
    e.nc = nc
    ctx = ExitStack()

    dt = nc.dram_tensor
    e.xT_in = dt("xT_in", [D, TPC], F32, kind="ExternalInput").ap()
    e.pos_in = dt("pos_in", [1, TPC], F32, kind="ExternalInput").ap()
    e.xT_out = dt("xT_out", [D, TPC], F32, kind="ExternalOutput").ap()
    for nm, shp in (("wqa_t", [D, D]), ("wqa_st", [D, D]),
                    ("wka_t", [512, D]), ("wka_st", [512, D]),
                    ("wva_t", [128, 16 * 512]), ("woa_t", [512, 16 * 512]),
                    ("wqb_t", [512, 16 * 512]), ("wkb_t", [512, 16 * 512]),
                    ("wvb_t", [512, 16 * 512]), ("wob_t", [512, 16 * 512])):
        setattr(e, nm, dt(nm, shp, F32, kind="ExternalInput").ap())
    for nm in ("lam_q1", "lam_q2", "lam_k1", "lam_k2"):
        setattr(e, nm, dt(nm, [1, 128], F32, kind="ExternalInput").ap())
    e.w_gate = dt("w_gate", [32 * 128, 16 * 256], I8, kind="ExternalInput").ap()
    e.w_up = dt("w_up", [32 * 128, 16 * 256], I8, kind="ExternalInput").ap()
    e.w_down = dt("w_down", [16 * 128, 64 * 128], I8, kind="ExternalInput").ap()
    e.s_gate = dt("s_gate", [64, 128], F32, kind="ExternalInput").ap()
    e.s_up = dt("s_up", [64, 128], F32, kind="ExternalInput").ap()
    e.s_down = dt("s_down", [16, 128], F32, kind="ExternalInput").ap()
    e.a2aq_i = dt("a2aq_i", [D, TPC], BF16, kind="Internal").ap()
    e.a2aq_o = dt("a2aq_o", [D, TPC], BF16, kind="Internal").ap()
    e.a2ak_i = dt("a2ak_i", [512, TPC], BF16, kind="Internal").ap()
    e.a2ak_o = dt("a2ak_o", [512, TPC], BF16, kind="Internal").ap()
    e.a2av_i = dt("a2av_i", [S, 64], BF16, kind="Internal").ap()
    e.a2av_o = dt("a2av_o", [S, 64], BF16, kind="Internal").ap()
    e.a2ao_i = dt("a2ao_i", [D, TPC], BF16, kind="Internal").ap()
    e.a2ao_o = dt("a2ao_o", [D, TPC], BF16, kind="Internal").ap()
    e.qb_dram = dt("qb_dram", [D, TPC], BF16, kind="Internal").ap()
    e.agk_i = dt("agk_i", [D, TPC], BF16, kind="Internal").ap()
    e.agk_o = dt("agk_o", [NCOR * D, TPC], BF16, kind="Internal").ap()
    e.agv_iA = dt("agv_iA", [TPC, 1024], BF16, kind="Internal").ap()
    e.agv_iB = dt("agv_iB", [TPC, 1024], BF16, kind="Internal").ap()
    e.agv_oA = dt("agv_oA", [S, 1024], BF16, kind="Internal").ap()
    e.agv_oB = dt("agv_oB", [S, 1024], BF16, kind="Internal").ap()
    e.a2an_i = dt("a2an_i", [S, 256], BF16, kind="Internal").ap()
    e.a2an_o = dt("a2an_o", [S, 256], BF16, kind="Internal").ap()

    sb = lambda name, shape, dtp: ctx.enter_context(nc.sbuf_tensor(name, shape, dtp))
    e.xT = sb("xT", [128, DCH * TPC], F32)
    e.hT = sb("hT", [128, DCH * TPC], BF16)
    e.ones_row = sb("ones_row", [128, 128], F32)
    e.ident64 = sb("ident64", [64, 64], F32)
    e.sg_sb = sb("sg_sb", [128, 64], F32)
    e.su_sb = sb("su_sb", [128, 64], F32)
    e.sd_sb = sb("sd_sb", [128, 16], F32)
    e.neg_lam = sb("neg_lam", [128, 1], F32)
    e.cbias = sb("cbias", [128, 4], F32)
    e.ones_bf = sb("ones_bf", [128, 1], BF16)
    e.stat = sb("stat", [1, 800], F32)

    bl = ctx.enter_context(nc.Block())
    pb = PB(nc, ctx, bl)
    e.pb = pb
    e.bl = bl
    e.ctx = ctx

    if "attn_a" in PHASES:
        with (nc.sbuf_tensor("cosq", [128, TPC], F32) as cosq,
              nc.sbuf_tensor("sinq", [128, TPC], F32) as sinq,
              nc.sbuf_tensor("ropescr", [128, 2048], F32) as ropescr):
            e.cosq, e.sinq, e.ropescr = cosq, sinq, ropescr
            emit_boot(e)
            emit_ln(e, "ln1")
            emit_attn_a(e)
    else:
        e.cosq = e.sinq = e.ropescr = None
        emit_boot(e)
    if "ln_dbg" in PHASES:
        emit_ln(e, "lnd")
        emit_hcopy(e)
    if "ffn1" in PHASES:
        emit_ln(e, "ln2")
        emit_ffn(e, "ffn1")
    if "attn_b" in PHASES:
        emit_ln(e, "ln3")
        emit_attn_b(e)
    if "ffn2" in PHASES:
        emit_ln(e, "ln4")
        emit_ffn(e, "ffn2")
    emit_store(e)
    pb.flush()

    ctx.close()
    return nc


def emit_boot(e):
    pb, nc = e.pb, e.nc
    bootctx = ExitStack()
    e.bootscr = bootctx.enter_context(nc.sbuf_tensor("bootscr", [64, 448], F32))

    def ld_x(eng, fin):
        for d in range(DCH):
            fin(eng.dma_start(out=e.xT[:, d * TPC:(d + 1) * TPC],
                              in_=e.xT_in[d * 128:(d + 1) * 128, :]))
        if "attn_a" in PHASES:
            fin(eng.dma_start(out=e.cosq[0:1, 0:1024], in_=e.pos_in))
        fin(eng.dma_start(out=e.bootscr[0:64, 0:128], in_=e.s_gate))
        fin(eng.dma_start(out=e.bootscr[0:64, 128:256], in_=e.s_up))
        fin(eng.dma_start(out=e.bootscr[0:16, 256:384], in_=e.s_down))
        for i, nm in enumerate(("lam_q1", "lam_k1", "lam_q2", "lam_k2")):
            fin(eng.dma_start(out=e.stat[0:1, i * 128:(i + 1) * 128], in_=getattr(e, nm)))
    e.tok_x = pb.step("y", ld_x, n_dma=DCH + 3 + 4 + (1 if "attn_a" in PHASES else 0))
    e.tok_lam_in = e.tok_x

    def consts_g0(eng, fin):
        fin(eng.iota(e.bootscr[0:64, 384:448], pattern=[[1, 64]], base=0, channel_multiplier=-1,
                     allow_small_or_imprecise_dtypes=True))
    tok_c0 = pb.step("g", consts_g0)

    def consts_g(eng, fin):
        fin(eng.tensor_scalar(e.ident64[:], e.bootscr[0:64, 384:448], 0.0, None, OP.is_equal))
    e.tok_consts = pb.step("g", consts_g, after=[tok_c0])

    def ones_v(eng, fin):
        eng.memset(e.ones_row[:], 1.0)
        eng.memset(e.ones_bf[:], 1.0)
        eng.memset(e.cbias[:, 0:1], EPS_LN)
        fin(eng.memset(e.cbias[:, 1:2], float(np.log(0.2))))
    e.tok_ones = pb.step("v", ones_v)

    with ExitStack() as bctx:
        en = bctx.enter_context
        ps1 = en(nc.psum_tensor("b_ps1", [128, 64], F32))
        ps2 = en(nc.psum_tensor("b_ps2", [128, 64], F32))
        ps3 = en(nc.psum_tensor("b_ps3", [128, 16], F32))
        ps_ang = en(nc.psum_tensor("b_ang", [128, TPC], F32))
        def sc_tr(eng, fin):
            eng.transpose(ps1[:], e.bootscr[0:64, 0:128], e.ident64[:])
            eng.transpose(ps2[:], e.bootscr[0:64, 128:256], e.ident64[:])
            fin(eng.transpose(ps3[:], e.bootscr[0:16, 256:384], e.ident64[0:16, 0:16]))
        tok_tr = pb.step("t", sc_tr, after=[e.tok_x, e.tok_consts])

        def sc_cp(eng, fin):
            eng.tensor_copy(e.sg_sb[:], ps1[:])
            eng.tensor_copy(e.su_sb[:], ps2[:])
            fin(eng.tensor_copy(e.sd_sb[:], ps3[:]))
        e.tok_scales = pb.step("v", sc_cp, after=[tok_tr])

        if "attn_a" in PHASES:
            def invf_iota(eng, fin):
                fin(eng.iota(e.stat[0:1, 512:640], pattern=[[0, 2], [1, 32], [0, 2]],
                             base=0, channel_multiplier=0,
                             allow_small_or_imprecise_dtypes=True))
            tok_ifi = pb.step("g", invf_iota, after=[e.tok_consts])

            def invf(eng, fin):
                fin(eng.activation(e.stat[0:1, 640:768], e.stat[0:1, 512:640], AF.Exp,
                                   scale=float(-2.0 * np.log(10000.0) / 64.0)))
            tok_if = pb.step("s", invf, after=[tok_ifi])

            def altc(eng, fin):
                fin(eng.iota(e.cbias[:, 2:3], pattern=[[0, 1]], base=0, channel_multiplier=1,
                             allow_small_or_imprecise_dtypes=True))
            tok_alt1 = pb.step("g", altc, after=[e.tok_consts])

            p_col = e.cbias[:, 2:3]
            y_col = e.ropescr[:, 0:1]
            t_a = pb.step("v", lambda g_, fin: fin(g_.tensor_scalar(
                y_col, p_col, 0.5, -0.25, OP.mult, OP.add)), after=[tok_alt1, e.tok_ones])
            t_a = pb.step("v", lambda g_, fin: fin(g_.tensor_scalar(
                y_col, y_col, MAGIC, None, OP.add)), after=[t_a])
            t_a = pb.step("v", lambda g_, fin: fin(g_.tensor_scalar(
                y_col, y_col, MAGIC, None, OP.subtract)), after=[t_a])
            t_a = pb.step("v", lambda g_, fin: fin(g_.scalar_tensor_tensor(
                y_col, y_col, -2.0, p_col, OP.mult, OP.add)), after=[t_a])
            tok_alt = pb.step("v", lambda g_, fin: fin(g_.tensor_scalar(
                e.cbias[:, 3:4], y_col, 2.0, -1.0, OP.mult, OP.add)), after=[t_a])

            def angmm(eng, fin):
                ins = None
                for hf in range(2):
                    ins = eng.matmul(ps_ang[:, hf * 512:(hf + 1) * 512], e.stat[0:1, 640:768],
                                     e.cosq[0:1, hf * 512:(hf + 1) * 512], start=True, stop=True)
                fin(ins)
            tok_ang = pb.step("t", angmm, after=[tok_if, e.tok_x, tok_tr])

            def rr1(eng, fin):
                a = ps_ang[:]
                y, n = e.ropescr[:, 0:1024], e.ropescr[:, 1024:2048]
                eng.tensor_scalar(y, a, float(1 / (2 * PI)), MAGIC, OP.mult, OP.add)
                eng.tensor_scalar(n, y, MAGIC, None, OP.subtract)
                eng.scalar_tensor_tensor(y, n, float(-2 * PI), a, OP.mult, OP.add)
                fin(eng.tensor_scalar(e.sinq[:], y, PI_CLAMP, -PI_CLAMP, OP.min, OP.max))
            tok_rr1 = pb.step("v", rr1, after=[tok_ang, e.tok_scales, tok_alt])

            def dosin1(eng, fin):
                fin(eng.activation(e.sinq[:], e.sinq[:], AF.Sin))
            tok_sin1 = pb.step("s", dosin1, after=[tok_rr1])

            def rr2(eng, fin):
                a = ps_ang[:]
                y, n = e.ropescr[:, 0:1024], e.ropescr[:, 1024:2048]
                eng.tensor_scalar(e.sinq[:], e.sinq[:], e.cbias[:, 3:4], None, OP.mult)
                eng.tensor_scalar(y, a, float(1 / (2 * PI)), 0.25, OP.mult, OP.add)
                eng.tensor_scalar(y, y, MAGIC, None, OP.add)
                eng.tensor_scalar(n, y, MAGIC, None, OP.subtract)
                eng.scalar_tensor_tensor(y, n, float(-2 * PI), a, OP.mult, OP.add)
                eng.tensor_scalar(y, y, float(PI / 2), None, OP.add)
                fin(eng.tensor_scalar(e.cosq[:], y, PI_CLAMP, -PI_CLAMP, OP.min, OP.max))
            tok_rr2 = pb.step("v", rr2, after=[tok_sin1])

            def dosin2(eng, fin):
                fin(eng.activation(e.cosq[:], e.cosq[:], AF.Sin))
            e.tok_rope = pb.step("s", dosin2, after=[tok_rr2])
        else:
            e.tok_rope = None
        pb.flush()
    bootctx.close()

    e.tok_hfree = None
    e.tok_scr_free = e.tok_rope
    e.tok_psum_free = [e.tok_scales, e.tok_rope]


def emit_ln(e, tag):
    pb, nc = e.pb, e.nc
    with ExitStack() as lctx:
        en = lctx.enter_context
        ps_s1 = en(nc.psum_tensor(tag + "_s1", [1, 1024], F32))
        ps_s2 = en(nc.psum_tensor(tag + "_s2", [1, 1024], F32))
        ps_ab = en(nc.psum_tensor(tag + "_ab", [128, 1024], F32))
        ps_bb = en(nc.psum_tensor(tag + "_bb", [128, 1024], F32))
        lntmp = en(nc.sbuf_tensor(tag + "_tmp", [128, 1024], F32))
        lnrows = en(nc.sbuf_tensor(tag + "_rows", [1, 4096], F32))
        sq = [en(nc.sbuf_tensor(tag + "_sq1", [128, 1024], BF16)),
              en(nc.sbuf_tensor(tag + "_sq2", [128, 1024], BF16))]
        tok_sq = [None] * DCH
        tok_s2 = [None] * DCH
        for d in range(DCH):
            def fsq(eng, fin, d=d):
                fin(eng.activation(sq[d % 2][:], e.xT[:, d * TPC:(d + 1) * TPC], AF.Square))
            deps = [e.tok_x, e.tok_scr_free]
            if d >= 2:
                deps.append(tok_s2[d - 2])
            tok_sq[d] = pb.step("s", fsq, after=deps)

            def fs1(eng, fin, d=d):
                for hf in range(2):
                    eng.matmul(ps_s1[:, hf * 512:(hf + 1) * 512], e.ones_row[:, 0:1],
                               e.xT[:, d * TPC + hf * 512: d * TPC + (hf + 1) * 512],
                               start=(d == 0), stop=(d == DCH - 1))
                ins = None
                for hf in range(2):
                    ins = eng.matmul(ps_s2[:, hf * 512:(hf + 1) * 512], e.ones_bf[:, 0:1],
                                     sq[d % 2][:, hf * 512:(hf + 1) * 512],
                                     start=(d == 0), stop=(d == DCH - 1))
                fin(ins)
            fdep = [e.tok_x, e.tok_ones, tok_sq[d]]
            if d == 0:
                fdep += e.tok_psum_free
            tok_s2[d] = pb.step("t", fs1, after=fdep)

        R_a = lnrows[0:1, 0:1024]
        R_m = lnrows[0:1, 1024:2048]
        R_b = lnrows[0:1, 2048:3072]
        R_q = lnrows[0:1, 3072:4096]

        def fstat1(eng, fin):
            eng.tensor_scalar(R_m, ps_s1[:], float(1.0 / D), None, OP.mult)
            eng.tensor_scalar(R_q, ps_s2[:], float(1.0 / D), None, OP.mult)
            eng.tensor_tensor(R_a, R_m, R_m, OP.mult)
            fin(eng.tensor_tensor(R_q, R_q, R_a, OP.subtract))
        tok_v1 = pb.step("v", fstat1, after=[tok_s2[DCH - 1]])

        def fstat2(eng, fin):
            eng.activation(R_b, R_q, AF.Ln, bias=e.cbias[0:1, 0:1])
            fin(eng.activation(R_a, R_b, AF.Exp, scale=-0.5))
        tok_v2 = pb.step("s", fstat2, after=[tok_v1])

        def fstat3(eng, fin):
            eng.tensor_tensor(R_b, R_m, R_a, OP.mult)
            fin(eng.tensor_scalar(R_b, R_b, -1.0, None, OP.mult))
        tok_v3 = pb.step("v", fstat3, after=[tok_v2])

        def fbc(eng, fin):
            ins = None
            for hf in range(2):
                eng.matmul(ps_ab[:, hf * 512:(hf + 1) * 512], e.ones_row[0:1, 0:128],
                           R_a[:, hf * 512:(hf + 1) * 512], start=True, stop=True)
                ins = eng.matmul(ps_bb[:, hf * 512:(hf + 1) * 512], e.ones_row[0:1, 0:128],
                                 R_b[:, hf * 512:(hf + 1) * 512], start=True, stop=True)
            fin(ins)
        tok_bc = pb.step("t", fbc, after=[tok_v3])

        tok_hh = None
        for d in range(DCH):
            def fh(eng, fin, d=d):
                sl = slice(d * TPC, (d + 1) * TPC)
                eng.tensor_tensor(lntmp[:], e.xT[:, sl], ps_ab[:], OP.mult)
                fin(eng.tensor_tensor(e.hT[:, sl], lntmp[:], ps_bb[:], OP.add))
            tok_hh = pb.step("v", fh, after=[tok_bc, e.tok_hfree])
        e.tok_h = tok_hh
        e.tok_scr_free = tok_hh
        e.tok_psum_free = [tok_hh]
        pb.flush()


def emit_ffn(e, tag):
    pb, nc = e.pb, e.nc
    NB = 32
    THW = 256
    NTH = TPC // THW
    with ExitStack() as fctx:
        en = fctx.enter_context
        pg0 = en(nc.psum_tensor(tag + "_g0", [128, THW], F32))
        pg1 = en(nc.psum_tensor(tag + "_g1", [128, THW], F32))
        pu0 = en(nc.psum_tensor(tag + "_u0", [128, THW], F32))
        pu1 = en(nc.psum_tensor(tag + "_u1", [128, THW], F32))
        py0 = en(nc.psum_tensor(tag + "_y0", [128, THW], F32))
        py1 = en(nc.psum_tensor(tag + "_y1", [128, THW], F32))
        fuseT = en(nc.sbuf_tensor(tag + "_fuse", [128, 64 * THW], BF16))
        wg0 = en(nc.sbuf_tensor(tag + "_wg0", [128, 16 * 256], BF16))
        wg1 = en(nc.sbuf_tensor(tag + "_wg1", [128, 16 * 256], BF16))
        wu0 = en(nc.sbuf_tensor(tag + "_wu0", [128, 16 * 256], BF16))
        wu1 = en(nc.sbuf_tensor(tag + "_wu1", [128, 16 * 256], BF16))
        wd0 = en(nc.sbuf_tensor(tag + "_wd0", [128, 64 * 128], BF16))
        wd1 = en(nc.sbuf_tensor(tag + "_wd1", [128, 64 * 128], BF16))
        gate_bf = en(nc.sbuf_tensor(tag + "_gb", [128, 2 * THW], BF16))
        pgs, pus, pys = [pg0, pg1], [pu0, pu1], [py0, py1]
        wgs, wus, wds = [wg0, wg1], [wu0, wu1], [wd0, wd1]
        tok_band = {}
        tok_wd = {}
        tok_up = {}
        tok_fuse = {}
        tok_silu = {}
        tok_down = {}
        tok_xupd = {}

        def band_load(th, b):
            def fldb(eng, fin, b=b):
                fin(eng.dma_start(out=wgs[b % 2][:],
                                  in_=e.w_gate[b * 128:(b + 1) * 128, :]))
                fin(eng.dma_start(out=wus[b % 2][:],
                                  in_=e.w_up[b * 128:(b + 1) * 128, :]))
            lin = th * NB + b
            deps = [e.tok_h]
            if lin >= 2:
                lb2 = lin - 2
                deps.append(tok_up[(lb2 // NB, (lb2 % NB) * 2 + 1)])
            tok_band[(th, b)] = pb.step("g", fldb, after=deps, n_dma=2)

        def wd_load(th, dc):
            def fldd(eng, fin, dc=dc):
                fin(eng.dma_start(out=wds[dc % 2][:],
                                  in_=e.w_down[dc * 128:(dc + 1) * 128, :]))
            lin = th * 16 + dc
            deps = []
            if lin < 2:
                deps.append(e.tok_h)
            if lin >= 2:
                pl = lin - 2
                deps.append(tok_down[(pl // 16, pl % 16)])
            tok_wd[(th, dc)] = pb.step("g", fldd, after=deps, n_dma=1)

        for th in range(NTH):
            if th == 0:
                band_load(0, 0)
                band_load(0, 1)
            for b in range(NB):
                for ic in range(2):
                    c = b * 2 + ic
                    lc = th * 64 + c

                    def fmm(eng, fin, th=th, b=b, ic=ic, c=c):
                        pg, pu = pgs[c % 2], pus[c % 2]
                        wgb, wub = wgs[b % 2], wus[b % 2]
                        for d in range(DCH):
                            lsl = slice(d * 256 + ic * 128, d * 256 + (ic + 1) * 128)
                            tsl = slice(d * TPC + th * THW, d * TPC + (th + 1) * THW)
                            eng.matmul(pg[:], wgb[:, lsl], e.hT[:, tsl],
                                       start=(d == 0), stop=(d == DCH - 1))
                        ins = None
                        for d in range(DCH):
                            lsl = slice(d * 256 + ic * 128, d * 256 + (ic + 1) * 128)
                            tsl = slice(d * TPC + th * THW, d * TPC + (th + 1) * THW)
                            ins = eng.matmul(pu[:], wub[:, lsl], e.hT[:, tsl],
                                             start=(d == 0), stop=(d == DCH - 1))
                        fin(ins)
                    deps = [tok_band[(th, b)], e.tok_h]
                    if lc < 2:
                        deps += e.tok_psum_free
                    if lc >= 2:
                        pl = lc - 2
                        deps += [tok_fuse[(pl // 64, pl % 64)], tok_silu[(pl // 64, pl % 64)]]
                    tok_up[(th, c)] = pb.step("t", fmm, after=deps)

                    def fsilu(eng, fin, th=th, c=c):
                        fin(eng.activation(gate_bf[:, (c % 2) * THW:((c % 2) + 1) * THW],
                                           pgs[c % 2][:], AF.Silu, scale=e.sg_sb[:, c:c + 1]))
                    tok_silu[(th, c)] = pb.step("s", fsilu,
                                                after=[tok_up[(th, c)], e.tok_scales])

                    def ffuse(eng, fin, th=th, c=c):
                        fin(eng.scalar_tensor_tensor(
                            fuseT[:, c * THW:(c + 1) * THW], pus[c % 2][:],
                            e.su_sb[:, c:c + 1],
                            gate_bf[:, (c % 2) * THW:((c % 2) + 1) * THW],
                            OP.mult, OP.mult))
                    fdeps = [tok_silu[(th, c)]]
                    if th > 0 and c < 2:
                        fdeps.append(tok_down[(th - 1, 15)])
                    tok_fuse[(th, c)] = pb.step("v", ffuse, after=fdeps)
                nl = th * NB + b + 2
                if 2 <= nl < NTH * NB:
                    band_load(nl // NB, nl % NB)

            if th == 0:
                wd_load(0, 0)
                wd_load(0, 1)
            for dc in range(16):

                def fdn(eng, fin, th=th, dc=dc):
                    py = pys[dc % 2]
                    wdb = wds[dc % 2]
                    ins = None
                    for ic in range(64):
                        ins = eng.matmul(py[:], wdb[:, ic * 128:(ic + 1) * 128],
                                         fuseT[:, ic * THW:(ic + 1) * THW],
                                         start=(ic == 0), stop=(ic == 63))
                    fin(ins)
                deps = [tok_wd[(th, dc)], tok_fuse[(th, 63)]]
                lin = th * 16 + dc
                if lin >= 2:
                    pl = lin - 2
                    deps.append(tok_xupd[(pl // 16, pl % 16)])
                tok_down[(th, dc)] = pb.step("t", fdn, after=deps)

                def fxu(eng, fin, th=th, dc=dc):
                    xs = e.xT[:, dc * TPC + th * THW: dc * TPC + (th + 1) * THW]
                    fin(eng.scalar_tensor_tensor(xs, pys[dc % 2][:], e.sd_sb[:, dc:dc + 1],
                                                 xs, OP.mult, OP.add))
                tok_xupd[(th, dc)] = pb.step("v", fxu, after=[tok_down[(th, dc)], e.tok_x])
                nl = th * 16 + dc + 2
                if 2 <= nl < NTH * 16:
                    wd_load(nl // 16, nl % 16)

        e.tok_x = tok_xupd[(NTH - 1, 15)]
        e.tok_hfree = tok_up[(NTH - 1, 63)]
        e.tok_psum_free = [tok_silu[(NTH - 1, 63)], tok_fuse[(NTH - 1, 63)], e.tok_x]
        pb.flush()


def emit_attn_a(e):
    pb, nc = e.pb, e.nc
    with ExitStack() as a1ctx:
        en = a1ctx.enter_context
        qT = en(nc.sbuf_tensor("a_qT", [128, DCH * TPC], BF16))
        kT = en(nc.sbuf_tensor("a_kT", [128, 4 * TPC], BF16))
        wA0 = en(nc.sbuf_tensor("a_wA0", [128, 16 * 128], BF16))
        wA1 = en(nc.sbuf_tensor("a_wA1", [128, 16 * 128], BF16))
        wB0 = en(nc.sbuf_tensor("a_wB0", [128, 16 * 128], BF16))
        wB1 = en(nc.sbuf_tensor("a_wB1", [128, 16 * 128], BF16))
        wV = en(nc.sbuf_tensor("a_wV", [128, 16 * 512], BF16))
        stg_v = en(nc.sbuf_tensor("a_stv", [128, 2 * 512], BF16))
        psA0 = en(nc.psum_tensor("a_pA0", [128, 512], F32))
        psA1 = en(nc.psum_tensor("a_pA1", [128, 512], F32))
        psB0 = en(nc.psum_tensor("a_pB0", [128, 512], F32))
        psB1 = en(nc.psum_tensor("a_pB1", [128, 512], F32))
        wAs, wBs = [wA0, wA1], [wB0, wB1]
        psAs, psBs = [psA0, psA1], [psB0, psB1]
        NSG = 20
        tok_w = {}
        tok_proj = {}
        tok_rope = {}

        def wload(sg):
            if sg < 16:
                srcA, srcB, r0 = e.wqa_t, e.wqa_st, sg * 128
            else:
                srcA, srcB, r0 = e.wka_t, e.wka_st, (sg - 16) * 128

            def fw(eng, fin, srcA=srcA, srcB=srcB, r0=r0, sg=sg):
                fin(eng.dma_start(out=wAs[sg % 2][:], in_=srcA[r0:r0 + 128, :]))
                fin(eng.dma_start(out=wBs[sg % 2][:], in_=srcB[r0:r0 + 128, :]))
            deps = []
            if sg < 2:
                deps.append(e.tok_h)
            if sg >= 2:
                deps.append(tok_proj[(sg - 2, 1)])
            tok_w[sg] = pb.step("g", fw, after=deps, n_dma=2)

        def wvload(eng, fin):
            fin(eng.dma_start(out=wV[:], in_=e.wva_t))
        tok_wv = pb.step("g", wvload, after=[e.tok_h], n_dma=1)

        wload(0)
        wload(1)
        for sg in range(NSG):
            dst = qT if sg < 16 else kT
            for th in range(2):
                u = th

                def fp(eng, fin, sg=sg, th=th, u=u):
                    pA, pB = psAs[u % 2], psBs[u % 2]
                    tsl = lambda d: slice(d * TPC + th * 512, d * TPC + (th + 1) * 512)
                    lsl = lambda d: slice(d * 128, (d + 1) * 128)
                    for d in range(DCH):
                        eng.matmul(pA[:], wAs[sg % 2][:, lsl(d)], e.hT[:, tsl(d)],
                                   start=(d == 0), stop=(d == DCH - 1))
                    ins = None
                    for d in range(DCH):
                        ins = eng.matmul(pB[:], wBs[sg % 2][:, lsl(d)], e.hT[:, tsl(d)],
                                         start=(d == 0), stop=(d == DCH - 1))
                    fin(ins)
                deps = [tok_w[sg], e.tok_h]
                if sg == 0:
                    deps += e.tok_psum_free
                prev = (sg, u - 2) if u >= 2 else ((sg - 1, u) if sg > 0 else None)
                if prev in tok_rope:
                    deps.append(tok_rope[prev])
                tok_proj[(sg, u)] = pb.step("t", fp, after=deps)

                def fr(eng, fin, sg=sg, th=th, u=u, dst=dst):
                    pA, pB = psAs[u % 2], psBs[u % 2]
                    csl = slice(th * 512, (th + 1) * 512)
                    cos, sinp = e.cosq[:, csl], e.sinq[:, csl]
                    dqc = sg if sg < 16 else sg - 16
                    col0 = dqc * TPC + th * 512
                    t1 = e.ropescr[:, 0:512]
                    t2 = e.ropescr[:, 512:1024]
                    eng.tensor_tensor(t1, pA[:], cos, OP.mult)
                    eng.tensor_tensor(t2, pB[:], sinp, OP.mult)
                    fin(eng.tensor_tensor(dst[:, col0:col0 + 512], t1, t2, OP.add))
                tok_rope[(sg, u)] = pb.step("v", fr,
                                            after=[tok_proj[(sg, u)], e.tok_rope])
            if sg + 2 < NSG:
                wload(sg + 2)

        tok_vst = {}
        prev_vev = None
        for kt in range(8):
            def fv(eng, fin, kt=kt):
                pq = psAs[kt % 2]
                ins = None
                for d in range(DCH):
                    ins = eng.matmul(pq[:], e.hT[:, d * TPC + kt * 128: d * TPC + (kt + 1) * 128],
                                     wV[:, d * 512:(d + 1) * 512],
                                     start=(d == 0), stop=(d == DCH - 1))
                fin(ins)
            deps = [tok_wv, e.tok_h, tok_rope[(NSG - 1, 1)], tok_rope[(NSG - 1, 0)]]
            if kt >= 2:
                deps.append(tok_vst[kt - 2])
            tok_vp = pb.step("t", fv, after=deps)

            def fve(eng, fin, kt=kt):
                fin(eng.activation(stg_v[:, (kt % 2) * 512:(kt % 2 + 1) * 512],
                                   pqs_dummy[kt % 2][:] if False else psAs[kt % 2][:],
                                   AF.Copy))
            edeps = [tok_vp]
            if prev_vev is not None and kt >= 2:
                edeps.append(tok_vst[kt - 2])
            tok_vev = pb.step("s", fve, after=edeps)
            prev_vev = tok_vev

            def fvs(eng, fin, kt=kt):
                for k in range(NCOR):
                    fin(eng.dma_start(
                        out=e.a2av_i[k * TPC + kt * 128: k * TPC + (kt + 1) * 128, :],
                        in_=stg_v[:, (kt % 2) * 512 + k * 64:(kt % 2) * 512 + (k + 1) * 64]))
            tok_vst[kt] = pb.step("y", fvs, after=[tok_vev], n_dma=NCOR)

        def stq(eng, fin):
            for d in range(DCH):
                fin(eng.dma_start(out=e.a2aq_i[d * 128:(d + 1) * 128, :],
                                  in_=qT[:, d * TPC:(d + 1) * TPC]))
        tok_stq = pb.step("y", stq, after=[tok_rope[(sgg, uu)] for sgg in range(16)
                                           for uu in range(2)], n_dma=DCH)

        def stk(eng, fin):
            for d in range(4):
                fin(eng.dma_start(out=e.a2ak_i[d * 128:(d + 1) * 128, :],
                                  in_=kT[:, d * TPC:(d + 1) * TPC]))
        tok_stk = pb.step("y", stk, after=[tok_rope[(sgg, uu)] for sgg in (16, 17, 18, 19)
                                           for uu in range(2)], n_dma=4)

        def collq(eng, fin):
            fin(eng.collective_compute("AllToAll", OP.bypass,
                                       replica_groups=[list(range(NCOR))],
                                       ins=[e.a2aq_i], outs=[e.a2aq_o]))
        tok_ccq = pb.step("g", collq, after=[tok_stq], cc=True)

        def collk(eng, fin):
            fin(eng.collective_compute("AllToAll", OP.bypass,
                                       replica_groups=[list(range(NCOR))],
                                       ins=[e.a2ak_i], outs=[e.a2ak_o]))
        tok_cck = pb.step("g", collk, after=[tok_stk], cc=True)

        def collv(eng, fin):
            fin(eng.collective_compute("AllToAll", OP.bypass,
                                       replica_groups=[list(range(NCOR))],
                                       ins=[e.a2av_i], outs=[e.a2av_o]))
        tok_ccv = pb.step("g", collv, after=[tok_vst[kk] for kk in range(8)], cc=True)
        e.tok_hfree = tok_vp
        e.tok_psum_free = [tok_rope[(NSG - 1, 1)], prev_vev]
        pb.flush()

    with ExitStack() as a2ctx:
        en = a2ctx.enter_context
        kcache = en(nc.sbuf_tensor("a_kc", [128, S], BF16))
        vfull = en(nc.sbuf_tensor("a_vf", [128, 64 * 65], BF16))
        qtiles = en(nc.sbuf_tensor("a_qt", [128, 4 * 512], BF16))
        pring = en(nc.sbuf_tensor("a_pr", [128, 6 * 1024], BF16))
        attnT = en(nc.sbuf_tensor("a_att", [64, 4 * 1024], BF16))
        att_f32 = en(nc.sbuf_tensor("a_af", [65, 1024], F32))
        rec_sc = en(nc.sbuf_tensor("a_rec", [128, 1024], F32))
        S0 = en(nc.psum_tensor("a_S0", [128, 1024], F32))
        S1 = en(nc.psum_tensor("a_S1", [128, 1024], F32))
        O0 = en(nc.psum_tensor("a_O0", [65, 512], F32))
        O1 = en(nc.psum_tensor("a_O1", [65, 512], F32))
        p_rb = en(nc.psum_tensor("a_rb", [64, 1024], F32))
        Sps = [S0, S1]
        NKT = [4 * (qc + 1) for qc in range(16)]

        def kcl(eng, fin):
            for half in range(2):
                for j in range(NCOR):
                    fin(eng.dma_start(out=kcache[half * 64:(half + 1) * 64, j * TPC:(j + 1) * TPC],
                                      in_=e.a2ak_o[j * 64:(j + 1) * 64, :]))
        tok_kc = pb.step("y", kcl, after=[tok_cck, tok_ccq, tok_ccv], n_dma=16)

        def vfl(eng, fin):
            for qq in range(4):
                src = e.a2av_o.rearrange("(kt p) j -> p kt j", p=128)[:, qq * 16:(qq + 1) * 16, :]
                dst = vfull[:].rearrange("p (kt j) -> p kt j", kt=64)[:, qq * 16:(qq + 1) * 16, 0:64]
                fin(eng.dma_start(out=dst, in_=src))
        tok_vf = pb.step("y", vfl, after=[tok_ccv, tok_ccq, tok_cck], n_dma=4)

        def vone(eng, fin):
            fin(eng.memset(vfull[:].rearrange("p (kt j) -> p kt j", kt=64)[:, :, 64:65], 1.0))
        tok_vones = pb.step("v", vone, after=[tok_ccq, tok_cck, tok_ccv])

        tok_qt = {}
        tok_S = {}
        tok_exp = {}
        tok_msk = {}
        tok_AV = {}
        tok_oev = {}
        tok_norm = {}
        tok_sto = {}
        prev_rc = None
        nkt_all = 0
        pend_av = [None]
        av_uid = [0]

        def kts_of(qcv, pair):
            return 0

        for qc in range(16):
            for pair in range(2):
                u = qc * 2 + pair

                def fqt(eng, fin, qc=qc, pair=pair, u=u):
                    fin(eng.dma_start(
                        out=qtiles[:, (u % 4) * 512:(u % 4 + 1) * 512],
                        in_=e.a2aq_o[(qc // 2) * 256 + pair * 128:(qc // 2) * 256 + (pair + 1) * 128,
                                     (qc % 2) * 512:((qc % 2) + 1) * 512]))
                deps = [tok_ccq, tok_cck, tok_ccv]
                if u >= 4:
                    u4 = u - 4
                    deps.append(tok_S[(u4, NKT[u4 // 2] - 1)])
                tok_qt[u] = pb.step("y", fqt, after=deps, n_dma=1)

                def emit_av(qcv, ktv, nv):
                    def fav(eng, fin, qc=qcv, kt=ktv, nkt_all=nv):
                        P = pring[:, (nkt_all % 6) * 1024:(nkt_all % 6 + 1) * 1024]
                        vt = vfull[:, kt * 65:(kt + 1) * 65]
                        eng.matmul(O0[:], vt, P[:, 0:512],
                                   start=(kt == 0), stop=(kt == NKT[qcv] - 1))
                        fin(eng.matmul(O1[:], vt, P[:, 512:1024],
                                       start=(kt == 0), stop=(kt == NKT[qcv] - 1)))
                    diagv = ktv >= NKT[qcv] - 4
                    deps = [tok_msk[nv] if diagv else tok_exp[nv], tok_vf, tok_vones]
                    if nv == 0:
                        deps += e.tok_psum_free
                    if ktv == 0 and av_uid[0] > 0:
                        deps.append(tok_oev[av_uid[0] - 1])
                    tok_AV[nv] = pb.step("t", fav, after=deps)

                for kt in range(NKT[qc]):
                    def fS(eng, fin, u=u, kt=kt, nkt_all=nkt_all):
                        Sp = Sps[nkt_all % 2]
                        qt = qtiles[:, (u % 4) * 512:(u % 4 + 1) * 512]
                        eng.matmul(Sp[:, 0:512], kcache[0:64, kt * 128:(kt + 1) * 128],
                                   qt[0:64, :], start=True, stop=True, tile_position=(0, 0))
                        fin(eng.matmul(Sp[:, 512:1024], kcache[64:128, kt * 128:(kt + 1) * 128],
                                       qt[64:128, :], start=True, stop=True,
                                       tile_position=(64, 0)))
                    deps = [tok_qt[u], tok_kc]
                    if nkt_all < 2:
                        deps += e.tok_psum_free
                    if nkt_all >= 2:
                        deps.append(tok_exp[nkt_all - 2])
                    tok_S[(u, kt)] = pb.step("t", fS, after=deps)

                    def fexp(eng, fin, nkt_all=nkt_all):
                        Sp = Sps[nkt_all % 2]
                        P = pring[:, (nkt_all % 6) * 1024:(nkt_all % 6 + 1) * 1024]
                        fin(eng.activation(P, Sp[:], AF.Exp, scale=0.125))
                    deps = [tok_S[(u, kt)]]
                    if nkt_all >= 6:
                        deps.append(tok_AV[nkt_all - 6])
                    tok_exp[nkt_all] = pb.step("s", fexp, after=deps)

                    diag = kt >= NKT[qc] - 4
                    if diag:
                        def fmsk(eng, fin, qc=qc, kt=kt, nkt_all=nkt_all):
                            P = pring[:, (nkt_all % 6) * 1024:(nkt_all % 6 + 1) * 1024]
                            base = qc * 512 - kt * 128
                            eng.affine_select(P[:, 0:512], P[:, 0:512], pattern=[[1, 512]],
                                              compare_op=OP.is_ge, fill=0.0, base=base,
                                              channel_multiplier=-1)
                            fin(eng.affine_select(P[:, 512:1024], P[:, 512:1024],
                                                  pattern=[[1, 512]], compare_op=OP.is_ge,
                                                  fill=0.0, base=base, channel_multiplier=-1))
                        tok_msk[nkt_all] = pb.step("g", fmsk, after=[tok_exp[nkt_all]])

                    if pend_av[0] is not None:
                        emit_av(*pend_av[0])
                    pend_av[0] = (qc, kt, nkt_all)
                    nkt_all += 1
                emit_av(*pend_av[0])
                pend_av[0] = None
                av_uid[0] = u + 1

                def foe(eng, fin):
                    eng.tensor_copy(att_f32[0:65, 0:512], O0[:])
                    fin(eng.tensor_copy(att_f32[0:65, 512:1024], O1[:]))
                deps = [tok_AV[nkt_all - 1]]
                if u > 0:
                    deps.append(tok_norm[u - 1])
                tok_oev[u] = pb.step("v", foe, after=deps)

                def frb(eng, fin):
                    eng.matmul(p_rb[:, 0:512], e.ones_row[64:65, 0:64],
                               att_f32[64:65, 0:512], start=True, stop=True)
                    fin(eng.matmul(p_rb[:, 512:1024], e.ones_row[64:65, 0:64],
                                   att_f32[64:65, 512:1024], start=True, stop=True))
                bdeps = [tok_oev[u]]
                if u == 0:
                    bdeps += e.tok_psum_free
                if prev_rc is not None:
                    bdeps.append(prev_rc)
                tok_rb = pb.step("t", frb, after=bdeps)

                def frc(eng, fin):
                    fin(eng.reciprocal(rec_sc[0:64, 0:1024], p_rb[:]))
                tok_rc = pb.step("v", frc, after=[tok_rb])
                prev_rc = tok_rc

                def fnm(eng, fin, qc=qc, pair=pair):
                    c0 = (qc % 2) * 512
                    eng.tensor_tensor(attnT[0:64, (2 * pair) * 1024 + c0:(2 * pair) * 1024 + c0 + 512],
                                      att_f32[0:64, 0:512], rec_sc[0:64, 0:512], OP.mult)
                    fin(eng.tensor_tensor(attnT[0:64, (2 * pair + 1) * 1024 + c0:(2 * pair + 1) * 1024 + c0 + 512],
                                          att_f32[0:64, 512:1024], rec_sc[0:64, 512:1024], OP.mult))
                ndeps = [tok_rb]
                if qc % 2 == 0 and qc >= 2:
                    ndeps.append(tok_sto[qc // 2 - 1])
                tok_norm[u] = pb.step("v", fnm, after=ndeps)

                if pair == 1 and qc % 2 == 1:
                    def fsto(eng, fin, qc=qc):
                        k = qc // 2
                        for h in range(4):
                            fin(eng.dma_start(
                                out=e.a2ao_i[256 * k + h * 64: 256 * k + (h + 1) * 64, :],
                                in_=attnT[0:64, h * 1024:(h + 1) * 1024]))
                    tok_sto[qc // 2] = pb.step("y", fsto,
                                               after=[tok_norm[qc * 2], tok_norm[qc * 2 + 1]],
                                               n_dma=4)

        def collo(eng, fin):
            fin(eng.collective_compute("AllToAll", OP.bypass,
                                       replica_groups=[list(range(NCOR))],
                                       ins=[e.a2ao_i], outs=[e.a2ao_o]))
        tok_cco = pb.step("g", collo, after=[tok_sto[NCOR - 1]], cc=True)
        e.tok_psum_free = [tok_norm[31], tok_oev[31]]
        pb.flush()

    with ExitStack() as a3ctx:
        en = a3ctx.enter_context
        attnq = en(nc.sbuf_tensor("a_aq", [128, DCH * TPC], BF16))
        wo0 = en(nc.sbuf_tensor("a_wo0", [128, 16 * 512], BF16))
        wo1 = en(nc.sbuf_tensor("a_wo1", [128, 16 * 512], BF16))
        po0 = en(nc.psum_tensor("a_po0", [128, 512], F32))
        po1 = en(nc.psum_tensor("a_po1", [128, 512], F32))
        wobufs = [wo0, wo1]
        pos_ = [po0, po1]

        def aql(eng, fin):
            for d in range(DCH):
                fin(eng.dma_start(out=attnq[:, d * TPC:(d + 1) * TPC],
                                  in_=e.a2ao_o[d * 128:(d + 1) * 128, :]))
        tok_aq = pb.step("y", aql, after=[tok_cco], n_dma=DCH)

        tok_wog = {}
        tok_womm = {}
        tok_xu = {}

        def wo_load(g):
            def fwo(eng, fin, g=g):
                fin(eng.dma_start(out=wobufs[g % 2][:],
                                  in_=e.woa_t[g * 128:(g + 1) * 128, :]))
            deps = []
            if g < 2:
                deps += e.tok_psum_free
                deps.append(tok_cco)
            if g >= 2:
                deps.append(tok_womm[(g - 2, 3, 1)])
            tok_wog[g] = pb.step("g", fwo, after=deps, n_dma=1)

        wo_load(0)
        wo_load(1)
        for g in range(4):
            for dc in range(4):
                for th in range(2):
                    uu = dc * 2 + th
                    lin = g * 8 + uu

                    def fwm(eng, fin, g=g, dc=dc, th=th, uu=uu):
                        po = pos_[uu % 2]
                        ins = None
                        for d in range(DCH):
                            ins = eng.matmul(po[:], wobufs[g % 2][:, d * 512 + dc * 128: d * 512 + (dc + 1) * 128],
                                             attnq[:, d * TPC + th * 512: d * TPC + (th + 1) * 512],
                                             start=(d == 0), stop=(d == DCH - 1))
                        fin(ins)
                    deps = [tok_wog[g], tok_aq]
                    if lin < 2:
                        deps += e.tok_psum_free
                    if lin >= 2:
                        pl = lin - 2
                        deps.append(tok_xu[pl])
                    tok_womm[(g, dc, th)] = pb.step("t", fwm, after=deps)

                    def fxu(eng, fin, g=g, dc=dc, th=th, uu=uu):
                        gd = g * 4 + dc
                        xs = e.xT[:, gd * TPC + th * 512: gd * TPC + (th + 1) * 512]
                        fin(eng.tensor_tensor(xs, xs, pos_[uu % 2][:], OP.add))
                    tok_xu[lin] = pb.step("v", fxu, after=[tok_womm[(g, dc, th)], e.tok_x])
            if g + 2 < 4:
                wo_load(g + 2)
        e.tok_x = tok_xu[31]
        e.tok_psum_free = [e.tok_x]
        pb.flush()


def emit_attn_b(e):
    pb, nc = e.pb, e.nc
    SC_B = float(1.0 / np.sqrt(128.0))
    abctx = ExitStack()
    kb1 = abctx.enter_context(nc.sbuf_tensor("b_k1", [128, 32 * 128], BF16))
    kb2 = abctx.enter_context(nc.sbuf_tensor("b_k2", [128, 32 * 128], BF16))
    qd1 = abctx.enter_context(nc.sbuf_tensor("b_qd1", [128, 1024], BF16))
    qd2 = abctx.enter_context(nc.sbuf_tensor("b_qd2", [128, 1024], BF16))
    with ExitStack() as b1ctx:
        en = b1ctx.enter_context
        w0 = en(nc.sbuf_tensor("b_w0", [128, 16 * 512], BF16))
        w1 = en(nc.sbuf_tensor("b_w1", [128, 16 * 512], BF16))
        stg = en(nc.sbuf_tensor("b_st", [128, 4 * 512], BF16))
        vloc = en(nc.sbuf_tensor("b_vl", [128, 8 * 2048], BF16))
        pq0 = en(nc.psum_tensor("b_pq0", [128, 512], F32))
        pq1 = en(nc.psum_tensor("b_pq1", [128, 512], F32))
        ps_lam = en(nc.psum_tensor("b_lam", [128, 1], F32))
        wbufs = [w0, w1]
        pqs = [pq0, pq1]
        tok_w = {}
        tok_pj = {}
        tok_ev = {}
        tok_st = {}
        grp_src = [(e.wkb_t, i) for i in range(4)] + \
                  [(e.wqb_t, i) for i in range(4)] + \
                  [(e.wvb_t, i) for i in range(4)]

        def wload(gi):
            src, g = grp_src[gi]

            def fw(eng, fin, src=src, g=g, gi=gi):
                fin(eng.dma_start(out=wbufs[gi % 2][:],
                                  in_=src[g * 128:(g + 1) * 128, :]))
            deps = []
            if gi < 2:
                deps.append(e.tok_h)
            if gi >= 2:
                deps.append(tok_pj[(gi - 2, 7)])
            tok_w[gi] = pb.step("g", fw, after=deps, n_dma=1)

        tok_cc = {}

        def emit_agk():
            def agk(eng, fin):
                fin(eng.collective_compute("AllGather", OP.bypass,
                                           replica_groups=[list(range(NCOR))],
                                           ins=[e.agk_i], outs=[e.agk_o]))
            tok_cc["agk"] = pb.step("g", agk, after=[tok_st[(gg, uu)] for gg in range(4)
                                                     for uu in range(8)], cc=True)

        def emit_agvA():
            def agvA(eng, fin):
                fin(eng.collective_compute("AllGather", OP.bypass,
                                           replica_groups=[list(range(NCOR))],
                                           ins=[e.agv_iA], outs=[e.agv_oA]))
            tok_cc["agvA"] = pb.step("g", agvA, after=[tok_stvA[0]], cc=True)

        tok_stvA = [None]
        wload(0)
        wload(1)
        for gi in range(12):
            isv = gi >= 8
            for ch in range(4):
                for th in range(2):
                    u = ch * 2 + th
                    lin = gi * 8 + u

                    def fp(eng, fin, gi=gi, ch=ch, th=th, u=u, isv=isv):
                        pq = pqs[u % 2]
                        ins = None
                        for d in range(DCH):
                            if isv:
                                kt = ch * 2 + th
                                ins = eng.matmul(pq[:], e.hT[:, d * TPC + kt * 128: d * TPC + (kt + 1) * 128],
                                                 wbufs[gi % 2][:, d * 512:(d + 1) * 512],
                                                 start=(d == 0), stop=(d == DCH - 1))
                            else:
                                ins = eng.matmul(pq[:], wbufs[gi % 2][:, d * 512 + ch * 128: d * 512 + (ch + 1) * 128],
                                                 e.hT[:, d * TPC + th * 512: d * TPC + (th + 1) * 512],
                                                 start=(d == 0), stop=(d == DCH - 1))
                        fin(ins)
                    deps = [tok_w[gi], e.tok_h]
                    if gi == 0 and u < 2:
                        deps += e.tok_psum_free
                    prev = (gi, u - 2) if u >= 2 else ((gi - 1, 6 + u) if gi > 0 else None)
                    if prev in tok_st:
                        deps.append(tok_st[prev])
                    tok_pj[(gi, u)] = pb.step("t", fp, after=deps)

                    def fev(eng, fin, gi=gi, ch=ch, th=th, u=u, isv=isv):
                        if isv:
                            kt = ch * 2 + th
                            g = gi - 8
                            fin(eng.activation(vloc[:, kt * 2048 + g * 512: kt * 2048 + (g + 1) * 512],
                                               pqs[u % 2][:], AF.Copy))
                        else:
                            fin(eng.activation(stg[:, (u % 4) * 512:(u % 4 + 1) * 512],
                                               pqs[u % 2][:], AF.Copy))
                    edeps = [tok_pj[(gi, u)]]
                    if not isv and lin >= 4:
                        pl = lin - 4
                        edeps.append(tok_st[(pl // 8, pl % 8)])
                    tok_ev[(gi, u)] = pb.step("s", fev, after=edeps)

                    if not isv:
                        def fst(eng, fin, gi=gi, ch=ch, th=th, u=u):
                            row = ((gi % 4) * 4 + ch) * 128
                            dram = e.agk_i if gi < 4 else e.qb_dram
                            fin(eng.dma_start(out=dram[row:row + 128, th * 512:(th + 1) * 512],
                                              in_=stg[:, (u % 4) * 512:(u % 4 + 1) * 512]))
                        tok_st[(gi, u)] = pb.step("y", fst, after=[tok_ev[(gi, u)]], n_dma=1)
                    else:
                        tok_st[(gi, u)] = tok_ev[(gi, u)]
            if gi + 2 < 12:
                wload(gi + 2)
            if gi == 3:
                emit_agk()
            if gi == 9:
                def stvA(eng, fin):
                    for kt in range(8):
                        fin(eng.dma_start(out=e.agv_iA[kt * 128:(kt + 1) * 128, :],
                                          in_=vloc[:, kt * 2048: kt * 2048 + 1024]))
                tok_stvA[0] = pb.step("y", stvA, after=[tok_ev[(gii, uu)] for gii in (8, 9)
                                                        for uu in range(8)], n_dma=8)
                emit_agvA()

        def stvB(eng, fin):
            for kt in range(8):
                fin(eng.dma_start(out=e.agv_iB[kt * 128:(kt + 1) * 128, :],
                                  in_=vloc[:, kt * 2048 + 1024:(kt + 1) * 2048]))
        tok_stvB = pb.step("y", stvB, after=[tok_ev[(gii, uu)] for gii in (10, 11)
                                             for uu in range(8)], n_dma=8)

        tok_agk = tok_cc["agk"]
        tok_agvA = tok_cc["agvA"]

        def agvB(eng, fin):
            fin(eng.collective_compute("AllGather", OP.bypass,
                                       replica_groups=[list(range(NCOR))],
                                       ins=[e.agv_iB], outs=[e.agv_oB]))
        tok_agvB = pb.step("g", agvB, after=[tok_stvB], cc=True)

        def flam1a(eng, fin):
            eng.tensor_tensor(e.stat[0:1, 512:640], e.stat[0:1, 0:128],
                              e.stat[0:1, 128:256], OP.mult)
            fin(eng.tensor_tensor(e.stat[0:1, 640:768], e.stat[0:1, 256:384],
                                  e.stat[0:1, 384:512], OP.mult))
        tok_l1a = pb.step("v", flam1a, after=[e.tok_lam_in, e.tok_rope])

        def flam1(eng, fin):
            eng.reduce_sum(e.stat[0:1, 768:769], e.stat[0:1, 512:640], axis=AX.X)
            fin(eng.reduce_sum(e.stat[0:1, 769:770], e.stat[0:1, 640:768], axis=AX.X))
        tok_l1 = pb.step("v", flam1, after=[tok_l1a])

        def flam2(eng, fin):
            fin(eng.activation(e.stat[0:1, 770:772], e.stat[0:1, 768:770], AF.Exp))
        tok_l2 = pb.step("s", flam2, after=[tok_l1])

        tok_l3a = pb.step("v", lambda g_, fin: fin(g_.tensor_tensor(
            e.stat[0:1, 772:773], e.stat[0:1, 770:771],
            e.stat[0:1, 771:772], OP.subtract)), after=[tok_l2])
        tok_l3 = pb.step("v", lambda g_, fin: fin(g_.tensor_scalar(
            e.stat[0:1, 772:773], e.stat[0:1, 772:773],
            0.8, None, OP.add)), after=[tok_l3a])

        def flam4(eng, fin):
            fin(eng.matmul(ps_lam[:], e.ones_row[0:1, 0:128], e.stat[0:1, 772:773],
                           start=True, stop=True))
        tok_l4 = pb.step("t", flam4, after=[tok_l3, e.tok_ones] + e.tok_psum_free)

        def flam5(eng, fin):
            fin(eng.tensor_scalar(e.neg_lam[:], ps_lam[:], -1.0, None, OP.mult))
        e.tok_lam = pb.step("v", flam5, after=[tok_l4])
        e.tok_hfree = tok_pj[(11, 7)]
        tok_qstaged = tok_st[(7, 7)]
        e.tok_psum_free = [tok_ev[(11, 7)], e.tok_lam]
        pb.flush()

    with ExitStack() as b2ctx:
        en = b2ctx.enter_context
        vbq = {}
        for vh in range(2):
            for par in range(2):
                vbq[(vh, par)] = en(nc.sbuf_tensor(f"b_v{vh}{par}", [128, 32 * 129], BF16))
        pring = en(nc.sbuf_tensor("b_pr", [128, 4 * 1024], BF16))
        o1sb = en(nc.sbuf_tensor("b_o1", [128, 2 * 4 * 129], F32))
        o2sb = en(nc.sbuf_tensor("b_o2", [128, 2 * 4 * 129], F32))
        normed = en(nc.sbuf_tensor("b_nm", [128, 8 * 2048], BF16))
        cscr = en(nc.sbuf_tensor("b_cs", [128, 2 * 4 * 128], F32))
        msq = en(nc.sbuf_tensor("b_msq", [128, 8], F32))
        cscr2 = en(nc.sbuf_tensor("b_cs2", [128, 4 * 128], F32))
        cs3 = en(nc.sbuf_tensor("b_cs3", [128, 32], F32))
        S0 = en(nc.psum_tensor("b_S0", [128, 1024], F32))
        S1 = en(nc.psum_tensor("b_S1", [128, 1024], F32))
        Oa = en(nc.psum_tensor("b_Oa", [128, 129], F32))
        Ob = en(nc.psum_tensor("b_Ob", [128, 129], F32))
        Oc = en(nc.psum_tensor("b_Oc", [128, 129], F32))
        Od = en(nc.psum_tensor("b_Od", [128, 129], F32))
        Sps = [S0, S1]
        Ops_ = [Oa, Ob, Oc, Od]
        kbufs = {1: kb1, 2: kb2}

        def vmemset(eng, fin):
            ins = None
            for b in vbq.values():
                ins = eng.memset(b[:].rearrange("p (u j) -> p u j", u=32)[:, :, 128:129], 1.0)
            fin(ins)
        tok_vone = pb.step("v", vmemset, after=[tok_agvB])

        tok_kld = {}
        tok_qd = {}
        tok_vld = {}
        tok_S = {}
        tok_exp = {}
        tok_AV = {}
        tok_o1ev = {}
        tok_o2ev = {}
        tok_cmb = {}
        ns2key = {}
        prev_o2ev = None
        prev_c2 = [None]
        pend_a = [None]
        pend_b = [None]
        chst = [None]
        nS = 0
        pend_avb = [None]
        def vload(hh, vh):
            def fvl(eng, fin, hh=hh, vh=vh):
                agv = e.agv_oA if hh < 8 else e.agv_oB
                c0 = (hh % 8) * 128
                for gg in range(2):
                    G = vh * 2 + gg
                    src = agv[G * 2048:(G + 1) * 2048, c0:c0 + 128] \
                        .rearrange("(kt p) c -> p kt c", p=128)
                    fin(eng.dma_start(
                        out=vbq[(vh, hh % 2)][:].rearrange("p (u j) -> p u j", u=32)
                        [:, gg * 16:(gg + 1) * 16, 0:128],
                        in_=src))
            deps = [tok_agvA if hh < 8 else tok_agvB, tok_vone]
            if hh >= 2:
                deps.append(tok_AV[(hh - 2, 2, 1, 31)])
            tok_vld[(hh, vh)] = pb.step("y", fvl, after=deps, n_dma=2)

        for h in range(16):
            m, par = h // 2, h % 2
            if h == 0:
                for hh in (0, 1):
                    vload(hh, 0)
                    vload(hh, 1)
            elif h + 1 < 16:
                vload(h + 1, 0)
                vload(h + 1, 1)
            for half in (1, 2):
                def fkl(eng, fin, m=m, par=par, half=half):
                    base = m * 256 + (half - 1) * 128 + par * 64
                    kv = e.agk_o.rearrange("(r f) t -> f r t", r=NCOR)
                    for lo in range(2):
                        fin(eng.dma_start(
                            out=kbufs[half][lo * 64:(lo + 1) * 64, :]
                            .rearrange("f (r t) -> f r t", r=4),
                            in_=kv[base:base + 64, lo * 4:(lo + 1) * 4, :]))
                kdeps = [tok_agk]
                if (h - 1, half) in tok_kld:
                    kdeps.append(tok_S[(h - 1, half, 1, 31)])
                tok_kld[(h, half)] = pb.step("y", fkl, after=kdeps, n_dma=2)

                def fqd(eng, fin, m=m, par=par, half=half):
                    row = m * 256 + (half - 1) * 128 + par * 64
                    qd = qd1 if half == 1 else qd2
                    fin(eng.dma_start(out=qd[0:64, :], in_=e.qb_dram[row:row + 64, :]))
                    fin(eng.dma_start(out=qd[64:128, :], in_=e.qb_dram[row:row + 64, :]))
                qdeps = [tok_qstaged]
                if (h - 1, half) in tok_qd:
                    qdeps.append(tok_S[(h - 1, half, 1, 31)])
                tok_qd[(h, half)] = pb.step("y", fqd, after=qdeps, n_dma=2)

            for qc in range(2):
                for half in (1, 2):
                    if half == 1 and pend_a[0] is not None:
                        pend_a[0]()
                        pend_a[0] = None

                    def emit_avb(hv, halfv, qcv, tpv, nv):
                        def fav(eng, fin, hv=hv, tp=tpv, nS=nv):
                            P = pring[:, (nS % 4) * 1024:(nS % 4 + 1) * 1024]
                            ins = None
                            for sub in range(2):
                                kt = tp * 2 + sub
                                vt = vbq[(sub, hv % 2)][:, tp * 129:(tp + 1) * 129]
                                for qs in range(4):
                                    ins = eng.matmul(
                                        Ops_[qs][:],
                                        P[:, sub * 512 + qs * 128: sub * 512 + (qs + 1) * 128],
                                        vt, start=(kt == 0), stop=(kt == 63))
                            fin(ins)
                        adeps = [tok_exp[nv]]
                        if tpv == 0:
                            adeps += [tok_vld[(hv, 0)], tok_vld[(hv, 1)]]
                        if nv == 0:
                            adeps += e.tok_psum_free
                        if tpv == 0:
                            if halfv == 2:
                                adeps.append(tok_o1ev[(hv, qcv)])
                            elif prev_o2ev is not None:
                                adeps.append(prev_o2ev)
                        tok_AV[(hv, halfv, qcv, tpv)] = pb.step("t", fav, after=adeps)

                    for tp in range(32):

                        def fS(eng, fin, half=half, qc=qc, tp=tp, nS=nS):
                            Sp = Sps[nS % 2]
                            kb = kbufs[half]
                            qd_ = qd1 if half == 1 else qd2
                            eng.matmul(Sp[:, 0:512], kb[0:64, tp * 128:(tp + 1) * 128],
                                       qd_[0:64, qc * 512:(qc + 1) * 512],
                                       start=True, stop=True, tile_position=(0, 0))
                            fin(eng.matmul(Sp[:, 512:1024], kb[64:128, tp * 128:(tp + 1) * 128],
                                           qd_[64:128, qc * 512:(qc + 1) * 512],
                                           start=True, stop=True, tile_position=(64, 0)))
                        deps = [tok_kld[(h, half)], tok_qd[(h, half)]]
                        if nS < 2:
                            deps += e.tok_psum_free
                        if nS >= 2:
                            deps.append(tok_exp[nS - 2])
                        tok_S[(h, half, qc, tp)] = pb.step("t", fS, after=deps)

                        def fexp(eng, fin, nS=nS):
                            Sp = Sps[nS % 2]
                            P = pring[:, (nS % 4) * 1024:(nS % 4 + 1) * 1024]
                            fin(eng.activation(P, Sp[:], AF.Exp, scale=SC_B))
                        edeps = [tok_S[(h, half, qc, tp)]]
                        if nS >= 4:
                            edeps.append(tok_AV[ns2key[nS - 4]])
                        tok_exp[nS] = pb.step("s", fexp, after=edeps)
                        ns2key[nS] = (h, half, qc, tp)

                        if pend_avb[0] is not None:
                            emit_avb(*pend_avb[0])
                        pend_avb[0] = (h, half, qc, tp, nS)
                        nS += 1
                    emit_avb(*pend_avb[0])
                    pend_avb[0] = None

                    pidx = h * 2 + qc
                    pb_ = (pidx % 2)
                    ob1 = o1sb[:, pb_ * 516:(pb_ + 1) * 516]
                    ob2 = o2sb[:, pb_ * 516:(pb_ + 1) * 516]
                    cs = cs3[:, pb_ * 16:(pb_ + 1) * 16]
                    if half == 1:
                        def fo1(eng, fin, ob1=ob1):
                            ins = None
                            for qs in range(4):
                                ins = eng.tensor_copy(ob1[:, qs * 129:(qs + 1) * 129],
                                                      Ops_[qs][:])
                            fin(ins)
                        tok_o1ev[(h, qc)] = pb.step("v", fo1,
                                                    after=[tok_AV[(h, 1, qc, 31)]])
                    else:
                        def fo2(eng, fin, ob2=ob2):
                            ins = None
                            for qs in range(4):
                                ins = eng.tensor_copy(ob2[:, qs * 129:(qs + 1) * 129],
                                                      Ops_[qs][:])
                            fin(ins)
                        tok_o2ev[(h, qc)] = pb.step("v", fo2,
                                                    after=[tok_AV[(h, 2, qc, 31)]])
                        prev_o2ev = tok_o2ev[(h, qc)]

                        def emit_chain_a(h=h, qc=qc, ob1=ob1, ob2=ob2, cs=cs, pb_=pb_):
                            csc = cscr[:, pb_ * 512:(pb_ + 1) * 512]
                            ms = msq[:, pb_ * 4:(pb_ + 1) * 4]
                            def fc1a(eng, fin):
                                ins = None
                                for qs in range(4):
                                    eng.reciprocal(cs[:, qs * 4:qs * 4 + 1],
                                                   ob1[:, qs * 129 + 128: qs * 129 + 129])
                                    ins = eng.reciprocal(cs[:, qs * 4 + 1:qs * 4 + 2],
                                                         ob2[:, qs * 129 + 128: qs * 129 + 129])
                                fin(ins)
                            tok_c1a = pb.step("v", fc1a,
                                              after=[tok_o2ev[(h, qc)], e.tok_lam])

                            def fc1b(eng, fin):
                                ins = None
                                for qs in range(4):
                                    ins = eng.tensor_tensor(cs[:, qs * 4 + 2:qs * 4 + 3],
                                                            cs[:, qs * 4 + 1:qs * 4 + 2],
                                                            e.neg_lam[:], OP.mult)
                                fin(ins)
                            tok_c1b = pb.step("g", fc1b, after=[tok_c1a])

                            def fc1c(eng, fin):
                                ins = None
                                for qs in range(4):
                                    ins = eng.tensor_scalar(cscr2[:, qs * 128:(qs + 1) * 128],
                                                            ob1[:, qs * 129: qs * 129 + 128],
                                                            cs[:, qs * 4:qs * 4 + 1], None, OP.mult)
                                fin(ins)
                            c1cdeps = [tok_c1b]
                            if prev_c2[0] is not None:
                                c1cdeps.append(prev_c2[0])
                            tok_c1c = pb.step("v", fc1c, after=c1cdeps)

                            def fc1d(eng, fin):
                                ins = None
                                for qs in range(4):
                                    ins = eng.scalar_tensor_tensor(
                                        csc[:, qs * 128:(qs + 1) * 128],
                                        ob2[:, qs * 129: qs * 129 + 128],
                                        cs[:, qs * 4 + 2:qs * 4 + 3],
                                        cscr2[:, qs * 128:(qs + 1) * 128], OP.mult, OP.add)
                                fin(ins)
                            tok_c1 = pb.step("v", fc1d, after=[tok_c1c])

                            def fc2(eng, fin):
                                ins = None
                                for qs in range(4):
                                    ins = eng.tensor_tensor(cscr2[:, qs * 128:(qs + 1) * 128],
                                                            csc[:, qs * 128:(qs + 1) * 128],
                                                            csc[:, qs * 128:(qs + 1) * 128],
                                                            OP.mult)
                                fin(ins)
                            tok_c2g = pb.step("g", fc2, after=[tok_c1])

                            def fc2r(eng, fin):
                                ins = None
                                for qs in range(4):
                                    ins = eng.reduce_sum(ms[:, qs:qs + 1],
                                                         cscr2[:, qs * 128:(qs + 1) * 128],
                                                         axis=AX.X)
                                fin(ins)
                            tok_c2 = pb.step("v", fc2r, after=[tok_c2g])
                            prev_c2[0] = tok_c2

                            def fc3(eng, fin):
                                ins = None
                                for qs in range(4):
                                    ins = eng.tensor_scalar(ms[:, qs:qs + 1],
                                                            ms[:, qs:qs + 1],
                                                            float(1.0 / 128.0), RMS_EPS,
                                                            OP.mult, OP.add)
                                fin(ins)
                            tok_c3 = pb.step("g", fc3, after=[tok_c2])
                            chst[0] = (tok_c3, ms, csc, cs, h, qc)

                        def emit_chain_b():
                            tok_c3, ms, csc, cs, h, qc = chst[0]

                            def fc4a(eng, fin):
                                ins = None
                                for qs in range(4):
                                    ins = eng.activation(cs[:, qs * 4 + 3:qs * 4 + 4],
                                                         ms[:, qs:qs + 1], AF.Ln)
                                fin(ins)
                            tok_c4a = pb.step("s", fc4a, after=[tok_c3])

                            def fc4(eng, fin):
                                ins = None
                                for qs in range(4):
                                    ins = eng.activation(cs[:, qs * 4 + 3:qs * 4 + 4],
                                                         cs[:, qs * 4 + 3:qs * 4 + 4], AF.Exp,
                                                         scale=-0.5, bias=e.cbias[:, 1:2])
                                fin(ins)
                            tok_c4 = pb.step("s", fc4, after=[tok_c4a])

                            def fc5(eng, fin):
                                ins = None
                                for qs in range(4):
                                    qt = qc * 4 + qs
                                    a0 = qt * 2048 + h
                                    outap = normed[:, a0: a0 + 16 * 127 + 1: 16]
                                    ins = eng.tensor_scalar(outap,
                                                            csc[:, qs * 128:(qs + 1) * 128],
                                                            cs[:, qs * 4 + 3:qs * 4 + 4],
                                                            None, OP.mult)
                                fin(ins)
                            tok_cmb[(h, qc)] = pb.step("v", fc5, after=[tok_c4])

                        if pend_b[0] is not None:
                            pend_b[0]()
                        pend_a[0] = emit_chain_a
                        pend_b[0] = emit_chain_b

        if pend_a[0] is not None:
            pend_a[0]()
            pend_a[0] = None
        if pend_b[0] is not None:
            pend_b[0]()
            pend_b[0] = None

        def stn(eng, fin):
            for k in range(NCOR):
                src = normed[:].rearrange("p (qt c) -> p qt c", qt=8)[:, :, 256 * k:256 * (k + 1)]
                dst = e.a2an_i.rearrange("(k qt p) c -> p qt k c", k=NCOR, qt=8)[:, :, k, :]
                fin(eng.dma_start(out=dst, in_=src))
        tok_stn = pb.step("y", stn, after=[tok_cmb[(hh, qq)] for hh in range(16)
                                           for qq in range(2)], n_dma=NCOR)

        def colln(eng, fin):
            fin(eng.collective_compute("AllToAll", OP.bypass,
                                       replica_groups=[list(range(NCOR))],
                                       ins=[e.a2an_i], outs=[e.a2an_o]))
        tok_ccn = pb.step("g", colln, after=[tok_stn], cc=True)
        e.tok_psum_free = [tok_cmb[(15, 1)], prev_o2ev]
        pb.flush()

    abctx.close()
    with ExitStack() as b3ctx:
        en = b3ctx.enter_context
        tTf = en(nc.sbuf_tensor("b_tt", [128, 8 * 16 * 128], BF16))
        wo0 = en(nc.sbuf_tensor("b_wo0", [128, 16 * 512], BF16))
        wo1 = en(nc.sbuf_tensor("b_wo1", [128, 16 * 512], BF16))
        py0 = en(nc.psum_tensor("b_py0", [128, 128], F32))
        py1 = en(nc.psum_tensor("b_py1", [128, 128], F32))
        wobufs = [wo0, wo1]
        pys = [py0, py1]

        def ttl(eng, fin):
            for ci in range(8):
                c, isub = ci // 2, ci % 2
                src = e.a2an_o[2048 * c: 2048 * (c + 1), isub * 128:(isub + 1) * 128] \
                    .rearrange("(jc p) i -> p jc i", p=128)
                dst = tTf[:].rearrange("p (ci jc i) -> p ci jc i", ci=8, jc=16)[:, ci]
                fin(eng.dma_start(out=dst, in_=src))
        tok_tt = pb.step("y", ttl, after=[tok_ccn], n_dma=8)

        tok_wog = {}
        tok_mm = {}
        tok_xu = {}

        def wo_load(g):
            def fwo(eng, fin, g=g):
                fin(eng.dma_start(out=wobufs[g % 2][:],
                                  in_=e.wob_t[g * 128:(g + 1) * 128, :]))
            deps = []
            if g < 2:
                deps += e.tok_psum_free
                deps.append(tok_ccn)
            if g >= 2:
                deps.append(tok_mm[(g - 2, 7, 3)])
            tok_wog[g] = pb.step("g", fwo, after=deps, n_dma=1)

        wo_load(0)
        wo_load(1)
        for g in range(4):
            for ci in range(8):
                for dc4 in range(4):
                    lin = (g * 8 + ci) * 4 + dc4

                    def fm(eng, fin, g=g, ci=ci, dc4=dc4, lin=lin):
                        py = pys[lin % 2]
                        ins = None
                        for jc in range(16):
                            ins = eng.matmul(
                                py[:], wobufs[g % 2][:, jc * 512 + dc4 * 128: jc * 512 + (dc4 + 1) * 128],
                                tTf[:, (ci * 16 + jc) * 128:(ci * 16 + jc + 1) * 128],
                                start=(jc == 0), stop=(jc == 15))
                        fin(ins)
                    deps = [tok_wog[g], tok_tt]
                    if lin < 2:
                        deps += e.tok_psum_free
                    if lin >= 2:
                        deps.append(tok_xu[lin - 2])
                    tok_mm[(g, ci, dc4)] = pb.step("t", fm, after=deps)

                    def fx(eng, fin, g=g, ci=ci, dc4=dc4, lin=lin):
                        c, isub = ci // 2, ci % 2
                        dc = g * 4 + dc4
                        base = dc * TPC + isub * 512 + c
                        xv = e.xT[:, base: base + 509]
                        xv = bass.AP(xv.tensor, xv.offset,
                                     [xv.ap[0], [64, 8], [4, 16]])
                        pv = pys[lin % 2][:].rearrange("p (a b) -> p a b", a=8)
                        fin(eng.tensor_tensor(xv, xv, pv, OP.add))
                    tok_xu[lin] = pb.step("v", fx, after=[tok_mm[(g, ci, dc4)], e.tok_x])
            if g + 2 < 4:
                wo_load(g + 2)
        e.tok_x = tok_xu[127]
        e.tok_psum_free = [e.tok_x]
        pb.flush()


def emit_hcopy(e):
    pb = e.pb

    def fc(eng, fin):
        ins = None
        for d in range(DCH):
            ins = eng.tensor_copy(e.xT[:, d * TPC:(d + 1) * TPC],
                                  e.hT[:, d * TPC:(d + 1) * TPC])
        fin(ins)
    e.tok_x = pb.step("v", fc, after=[e.tok_h, e.tok_x])
    pb.flush()


def emit_store(e):
    pb = e.pb
    dbg = os.environ.get("A_DBG", "")

    def st(eng, fin):
        for d in range(DCH):
            fin(eng.dma_start(out=e.xT_out[d * 128:(d + 1) * 128, :],
                              in_=e.xT[:, d * TPC:(d + 1) * TPC]))
    tok = pb.step("y", st, after=[e.tok_x], n_dma=DCH)

    if dbg:
        src_map = {"qo": e.a2aq_o, "ko": e.a2ak_o, "oi": e.a2ao_i, "oo": e.a2ao_o,
                   "vo": e.a2av_o.rearrange("(a b) c -> a (b c)", b=16),
                   "no": e.a2an_o.rearrange("(a b) c -> a (b c)", b=4) if "attn_b" in PHASES else None,
                   "ko2": e.agk_o.rearrange("(a b) c -> a (b c)", b=8) if "attn_b" in PHASES else None}
        if dbg == "tbl":
            def fdbg(eng, fin):
                fin(eng.dma_start(out=e.xT_out[0:128, :], in_=e.cosq[:]))
                fin(eng.dma_start(out=e.xT_out[128:256, :], in_=e.sinq[:]))
                fin(eng.dma_start(out=e.xT_out[256:384, 0:4], in_=e.cbias[:]))
            tok = pb.step("g", fdbg, after=[tok], n_dma=3)
        else:
            sap = src_map[dbg]

            def fdbg(eng, fin):
                r = sap.shape[0]
                fin(eng.dma_start(out=e.xT_out[0:r, :], in_=sap))
            tok = pb.step("g", fdbg, after=[tok], n_dma=1)

    def fwait(eng, fin):
        fin(eng.nop())
    pb.step("y", fwait, after=[tok])


def _t4(w, SG):
    return np.ascontiguousarray(
        np.asarray(w, np.float32).reshape(SG, 128, 16, 128)
        .transpose(0, 3, 2, 1).reshape(SG * 128, 2048))


def _t512(w):
    return np.ascontiguousarray(
        np.asarray(w, np.float32).reshape(4, 512, 16, 128)
        .transpose(0, 3, 2, 1).reshape(512, 8192))


def _prep_weights(inputs):
    wq = np.asarray(inputs["L0_attn_a_wq"], np.float32)
    wqs = wq.reshape(1024, 2, 2048)[:, ::-1, :].reshape(2048, 2048)
    wk = np.asarray(inputs["L0_attn_a_wk"], np.float32)
    wks = wk.reshape(256, 2, 2048)[:, ::-1, :].reshape(512, 2048)
    wv = np.asarray(inputs["L0_attn_a_wv"], np.float32)
    wva_t = np.ascontiguousarray(wv.reshape(512, 16, 128).transpose(2, 1, 0)
                                 .reshape(128, 8192))
    idx = np.arange(2048)
    m_, rem = idx // 256, idx % 256
    half, rem2 = rem // 128, rem % 128
    par, rr = rem2 // 64, rem2 % 64
    perm = (2 * m_ + par) * 128 + half * 64 + rr
    wg = np.asarray(inputs["L0_ffn_w_gate"])
    wu = np.asarray(inputs["L0_ffn_w_up"])
    wd = np.asarray(inputs["L0_ffn_w_down"])
    return {
        "wqa_t": _t4(wq, 16),
        "wqa_st": _t4(wqs, 16),
        "wka_t": _t4(wk, 4),
        "wka_st": _t4(wks, 4),
        "wva_t": wva_t,
        "woa_t": _t512(np.asarray(inputs["L0_attn_a_wo"], np.float32)),
        "wqb_t": _t512(np.asarray(inputs["L0_attn_b_wq"], np.float32)[perm]),
        "wkb_t": _t512(np.asarray(inputs["L0_attn_b_wk"], np.float32)[perm]),
        "wvb_t": _t512(np.asarray(inputs["L0_attn_b_wv"], np.float32)),
        "wob_t": _t512(np.asarray(inputs["L0_attn_b_wo"], np.float32)),
        "w_gate": np.ascontiguousarray(
            wg.reshape(16, 128, 32, 256).transpose(2, 1, 0, 3).reshape(4096, 4096)),
        "w_up": np.ascontiguousarray(
            wu.reshape(16, 128, 32, 256).transpose(2, 1, 0, 3).reshape(4096, 4096)),
        "w_down": np.ascontiguousarray(
            wd.reshape(64, 128, 16, 128).transpose(2, 1, 0, 3).reshape(2048, 8192)),
        "lam_q1": np.asarray(inputs["L0_attn_b_lambda_q1"]).reshape(1, 128),
        "lam_q2": np.asarray(inputs["L0_attn_b_lambda_q2"]).reshape(1, 128),
        "lam_k1": np.asarray(inputs["L0_attn_b_lambda_k1"]).reshape(1, 128),
        "lam_k2": np.asarray(inputs["L0_attn_b_lambda_k2"]).reshape(1, 128),
        "s_gate": np.asarray(inputs["L0_ffn_s_gate"]).reshape(64, 128),
        "s_up": np.asarray(inputs["L0_ffn_s_up"]).reshape(64, 128),
        "s_down": np.asarray(inputs["L0_ffn_s_down"]).reshape(16, 128),
    }


def kernel(**inputs):
    x = np.asarray(inputs["L0_attn_a_x"])
    shared = _prep_weights(inputs)
    in_maps = []
    for c in range(NCOR):
        xs = np.ascontiguousarray(x[0, c * TPC:(c + 1) * TPC, :].T)
        pos = np.arange(c * TPC, (c + 1) * TPC, dtype=np.float32).reshape(1, TPC)
        m = dict(shared)
        m["xT_in"] = xs
        m["pos_in"] = pos
        in_maps.append(m)

    nc = build_program()
    res = run_bass_kernel_spmd(nc, in_maps, core_ids=list(range(NCOR)))
    out = np.empty((1, S, D), dtype=np.float32)
    for c in range(NCOR):
        out[0, c * TPC:(c + 1) * TPC, :] = np.asarray(res.results[c]["xT_out"]).T
    return out
